# version 34
# baseline (speedup 1.0000x reference)
import math
import numpy as np
import ml_dtypes
from contextlib import ExitStack

import concourse.bass as bass
import concourse.mybir as mybir
from concourse.bass_utils import run_bass_kernel_spmd

F32 = mybir.dt.float32
BF16 = mybir.dt.bfloat16
U8 = mybir.dt.uint8
AF = mybir.ActivationFunctionType
ALU = mybir.AluOpType
AX = mybir.AxisListType

BFD = ml_dtypes.bfloat16

S = 2048
D = 2048
HID = 5632
NC = D // 128
NTB = S // 512
NHC = HID // 128
NPR = NHC // 2
MARGIN = 16.0
SUBS = 16

DEBUG_TAPS = False
EXP_DELAY = True
CONST_MHAT = False


def _bucket_of_delta(delta):
    NB, MD = 32, 128
    rp = np.maximum(delta, 0)
    mx = NB // 2
    small = rp < mx
    rpf = np.maximum(rp, 1).astype(np.float32)
    val = np.log(rpf / mx).astype(np.float32) / np.float32(math.log(MD / mx)) * (NB - mx)
    lg = mx + val.astype(np.int32)
    lg = np.minimum(lg, NB - 1)
    return np.where(small, rp, lg)


def _host_onehot_win():
    deltas = np.arange(-(S - 1), S)
    buckets = _bucket_of_delta(deltas)
    M = np.zeros((32, len(deltas)), np.float32)
    M[buckets, np.arange(len(deltas))] = 1.0
    M[0, :] -= 1.0
    return np.ascontiguousarray(M[:, 1920:2432]).astype(BFD)


class _PEWrap:
    def __init__(self, e):
        self._e = e

    def matmul(self, *a, **k):
        k.setdefault("skip_group_check", True)
        return self._e.matmul(*a, **k)

    def __getattr__(self, n):
        return getattr(self._e, n)


class Asm:
    def __init__(self, nc, sems):
        self.nc = nc
        self.sems = sems
        self.cnt = {s: 0 for s in sems}
        self.mile = {}
        self.ops = []
        self.uid = 0

    def new_block(self):
        self.ops = []

    def op(self, eng, fn, sem=None, n=1, waits=()):
        if sem is not None:
            self.cnt[sem] += n
        self.ops.append((eng, fn, sem, n, tuple(w for w in waits if w is not None)))

    def mark(self, sem):
        self.uid += 1
        name = f"m{self.uid}"
        self.mile[name] = (sem, self.cnt[sem])
        return name

    def emit(self, block):
        mile, sems, ops = self.mile, self.sems, self.ops

        def body(engine, eng_name):
            if eng_name == "pe":
                engine = _PEWrap(engine)
            waited = {}
            for (eng, fn, sem, n, waits) in ops:
                if eng != eng_name:
                    continue
                for w in waits:
                    s, v = mile[w] if isinstance(w, str) else w
                    if waited.get(s, -1) >= v:
                        continue
                    waited[s] = v
                    engine.wait_ge(sems[s], v)
                ins = fn(engine)
                if sem is not None:
                    ins.then_inc(sems[sem], n)

        for nm in ("gp", "sy", "pe", "act", "dve"):
            if any(o[0] == nm for o in ops):
                getattr(block, {"gp": "gpsimd", "sy": "sync", "pe": "tensor",
                                "act": "scalar", "dve": "vector"}[nm])(
                    lambda e, nm=nm: body(e, nm))


def build_program(detect_races=True):
    nc = bass.Bass("TRN2", target_bir_lowering=False,
                   detect_race_conditions=detect_races)
    es = ExitStack()

    dt_x = nc.dram_tensor("x", [NC, 128, S], F32, kind="ExternalInput").ap()
    dt_wg = nc.dram_tensor("wg", [NPR, 128, 16, 256], U8, kind="ExternalInput").ap()
    dt_wu = nc.dram_tensor("wu", [NPR, 128, 16, 256], U8, kind="ExternalInput").ap()
    dt_wd = nc.dram_tensor("wd", [NPR, 128, 2, D], U8, kind="ExternalInput").ap()
    dt_wq = nc.dram_tensor("wqa", [8, 128, 16, 256], F32, kind="ExternalInput").ap()
    dt_wk = nc.dram_tensor("wka", [8, 128, 16, 256], F32, kind="ExternalInput").ap()
    dt_wv = nc.dram_tensor("wva", [8, 128, 16, 256], F32, kind="ExternalInput").ap()
    dt_wo = nc.dram_tensor("woa", [16, 128, D], F32, kind="ExternalInput").ap()
    dt_rb = nc.dram_tensor("relb", [32, 32], F32, kind="ExternalInput").ap()
    dt_sg = nc.dram_tensor("sgsd", [128, NHC], F32, kind="ExternalInput").ap()
    dt_su = nc.dram_tensor("susd", [128, NHC], F32, kind="ExternalInput").ap()
    dt_sd = nc.dram_tensor("sdsd", [128, NC], F32, kind="ExternalInput").ap()
    dt_idb = nc.dram_tensor("idbd", [128, 128], BF16, kind="ExternalInput").ap()
    dt_ones = nc.dram_tensor("ones128d", [128, 1], BF16, kind="ExternalInput").ap()
    dt_onesr = nc.dram_tensor("onesrowd", [1, 128], BF16, kind="ExternalInput").ap()
    dt_neg = nc.dram_tensor("neg128rd", [1, 128], BF16, kind="ExternalInput").ap()
    dt_oh = nc.dram_tensor("onehotd", [32, 512], BF16, kind="ExternalInput").ap()
    dt_out = nc.dram_tensor("out", [NC, 128, NTB, 512], F32, kind="ExternalOutput").ap()
    d_toep = nc.dram_tensor("toepd", [32, 512], BF16, kind="Internal").ap()
    dt_tap1 = dt_tap2 = None
    DBG = {}
    if DEBUG_TAPS:
        dt_tap1 = nc.dram_tensor("tap1", [NC, 128, S], F32, kind="ExternalOutput").ap()
        dt_tap2 = nc.dram_tensor("tap2", [NC, 128, S], F32, kind="ExternalOutput").ap()
        DBG["xn"] = nc.dram_tensor("dbg_xn", [128, 8192], F32, kind="ExternalOutput").ap()
        DBG["hb"] = nc.dram_tensor("dbg_hb", [128, NHC * 512], F32, kind="ExternalOutput").ap()
        DBG["mr"] = nc.dram_tensor("dbg_mr", [1, S], F32, kind="ExternalOutput").ap()
        DBG["rs"] = nc.dram_tensor("dbg_rs", [1, S], F32, kind="ExternalOutput").ap()
        DBG["cs"] = nc.dram_tensor("dbg_cs", [1, 512], F32, kind="ExternalOutput").ap()
        DBG["var"] = nc.dram_tensor("dbg_var", [8, 512], F32, kind="ExternalOutput").ap()
        DBG["q"] = nc.dram_tensor("dbg_q", [128, 2, S], F32, kind="ExternalOutput").ap()
        DBG["k"] = nc.dram_tensor("dbg_k", [128, 2, S], F32, kind="ExternalOutput").ap()
        DBG["va"] = nc.dram_tensor("dbg_va", [128, 16, 260], F32, kind="ExternalOutput").ap()
        DBG["ab"] = nc.dram_tensor("dbg_ab", [128, NC, S], F32, kind="ExternalOutput").ap()
        DBG["bm"] = nc.dram_tensor("dbg_bm", [128, 4], F32, kind="ExternalOutput").ap()
        DBG["e0"] = nc.dram_tensor("dbg_e0", [128, 1024], F32, kind="ExternalOutput").ap()
        DBG["pv"] = nc.dram_tensor("dbg_pv", [128, 1024], F32, kind="ExternalOutput").ap()
        DBG["mx"] = nc.dram_tensor("dbg_mx", [128, 16], F32, kind="ExternalOutput").ap()

    xbuf = nc.alloc_sbuf_tensor("xbuf", [128, NC, S], BF16).ap()
    rstd_b = nc.alloc_sbuf_tensor("rstd_b", [128, S], BF16).ap()
    m_row = nc.alloc_sbuf_tensor("m_row", [128, S], BF16).ap()
    row_bf = nc.alloc_sbuf_tensor("row_bf", [128, 512], BF16).ap()
    scrA = nc.alloc_sbuf_tensor("scrA", [1, 512], F32).ap()
    scrB = nc.alloc_sbuf_tensor("scrB", [1, 512], F32).ap()
    r31row = nc.alloc_sbuf_tensor("r31row", [128, 32], BF16).ap()
    idb = nc.alloc_sbuf_tensor("idb", [128, 128], BF16).ap()
    ones128 = nc.alloc_sbuf_tensor("ones128", [128, 1], BF16).ap()
    onesr = nc.alloc_sbuf_tensor("onesr", [1, 128], BF16).ap()
    negr = nc.alloc_sbuf_tensor("negr", [1, 128], BF16).ap()
    negrz = nc.alloc_sbuf_tensor("negrz", [128, 128], BF16).ap()
    onesrz = nc.alloc_sbuf_tensor("onesrz", [128, 128], BF16).ap()
    sgs = nc.alloc_sbuf_tensor("sgs", [128, NHC], F32).ap()
    sus = nc.alloc_sbuf_tensor("sus", [128, NHC], F32).ap()
    sdh = nc.alloc_sbuf_tensor("sdh", [128, NC], F32).ap()
    rstd_col = nc.alloc_sbuf_tensor("rstd_col", [128, NC], F32).ap()
    mx1 = nc.alloc_sbuf_tensor("mx1", [128, 1], BF16).ap()
    bm = [nc.alloc_sbuf_tensor(f"bm{i}", [128, 1], F32).ap() for i in range(4)]
    bm31 = [nc.alloc_sbuf_tensor(f"bm31{i}", [128, 1], F32).ap() for i in range(4)]
    rrow_bf = nc.alloc_sbuf_tensor("rrow_bf", [128, 1024], BF16).ap()
    srow = nc.alloc_sbuf_tensor("srow", [128, 8], BF16).ap()
    c_eps = nc.alloc_sbuf_tensor("c_eps", [1, 1], F32).ap()
    c_mhalf = nc.alloc_sbuf_tensor("c_mhalf", [1, 1], F32).ap()

    pAll = nc.alloc_psum_tensor("pAll", [128, 3584], F32).ap()
    pTrb = nc.alloc_psum_tensor("pTrb", [128, 1024], BF16).ap()

    def bank(i, w=512):
        return pAll[:, i * 512: i * 512 + w]

    sems = {}
    for s in ("sp", "sa", "sv", "sc", "sx", "srb", "stw", "stp",
              "sw0", "sw1", "sw2", "sw3", "swq0", "swq1",
              "so0", "so1", "so2", "so3", "so4", "so5", "so6", "so7", "sdb", "sdt"):
        sems[s] = es.enter_context(nc.semaphore(s))
    A = Asm(nc, sems)
    M = A.mark

    slot_last = {}
    bank_last = {}
    mbst = {}

    def bank_waits(*bs):
        return [bank_last.get(b) for b in bs]

    def stats_pass(tag, sq_bufs, src_ready=None):
        sq_marks = {}
        def srcw(dc):
            if isinstance(src_ready, dict):
                return src_ready.get(dc)
            return src_ready
        for half in range(2):
            tbs = (2 * half, 2 * half + 1)
            grp_marks = []
            for dc in range(NC):
                sq = sq_bufs[dc % 2]
                A.op("act", lambda e, sq=sq, dc=dc, half=half: e.activation(
                    sq, xbuf[:, dc, half * 1024:(half + 1) * 1024], AF.Square),
                    sem="sa", waits=[srcw(dc), sq_marks.get(dc % 2)])
                msq = M("sa")
                for j, tb in enumerate(tbs):
                    bsum, bsq = 2 * j, 2 * j + 1
                    A.op("pe", lambda e, dc=dc, tb=tb, b=bsum: e.matmul(
                        pAll[0:1, b * 512:b * 512 + 512], ones128[:, 0:1],
                        xbuf[:, dc, tb * 512:(tb + 1) * 512],
                        start=(dc == 0), stop=(dc == NC - 1)),
                        sem="sp" if dc == NC - 1 else None,
                        waits=[srcw(dc)] + (bank_waits(bsum) if dc == 0 else []))
                    if dc == NC - 1:
                        grp_marks.append((bsum, M("sp")))
                    last_sq = (j == len(tbs) - 1)
                    A.op("pe", lambda e, sq=sq, tb=tb, b=bsq, dc=dc, half=half:
                         e.matmul(
                        pAll[0:1, b * 512:b * 512 + 512], ones128[:, 0:1],
                        sq[:, (tb - 2 * half) * 512:(tb - 2 * half + 1) * 512],
                        start=(dc == 0), stop=(dc == NC - 1)),
                        sem="sp" if (dc == NC - 1 or last_sq) else None,
                        waits=([msq] + bank_waits(bsq)) if dc == 0 else [msq])
                    if dc == NC - 1 and last_sq:
                        mk = M("sp"); grp_marks.append((bsq, mk)); sq_marks[dc % 2] = mk
                    elif dc == NC - 1:
                        grp_marks.append((bsq, M("sp")))
                    elif last_sq:
                        sq_marks[dc % 2] = M("sp")
            pe_done = grp_marks[-1][1]
            for j, tb in enumerate(tbs):
                bsum, bsq = 2 * j, 2 * j + 1
                sl = slice(tb * 512, (tb + 1) * 512)
                A.op("dve", lambda e, b=bsum: e.tensor_scalar(
                    scrA, pAll[0:1, b * 512:b * 512 + 512], 1.0 / 2048.0, None,
                    ALU.mult), sem="sv", waits=[pe_done])
                A.op("dve", lambda e, sl=sl: e.tensor_copy(m_row[0:1, sl], scrA),
                     sem="sv")
                mbst[f"mrow_{tag}_{tb}"] = M("sv")
                A.op("dve", lambda e: e.tensor_mul(scrB, scrA, scrA), sem="sv")
                A.op("dve", lambda e, b=bsq: e.tensor_scalar(
                    scrA, pAll[0:1, b * 512:b * 512 + 512], 1.0 / 2048.0, None,
                    ALU.mult), sem="sv")
                if DEBUG_TAPS and tag == "ln1":
                    A.op("dve", lambda e: e.tensor_copy(DBG["scr"], scrA),
                         sem="sv", waits=[mbst.get("dbgscr")])
                    dsc = M("sv")
                    A.op("gp", lambda e, tb=tb: e.dma_start(
                        out=DBG["var"][tb:tb + 1, :], in_=DBG["scr"]),
                        sem="sdb", n=16, waits=[dsc])
                    mbst["dbgscr"] = M("sdb")
                A.op("dve", lambda e: e.tensor_sub(scrA, scrA, scrB), sem="sv")
                vr = M("sv")
                if DEBUG_TAPS and tag == "ln1":
                    A.op("dve", lambda e: e.tensor_copy(DBG["scr"], scrA),
                         sem="sv", waits=[mbst.get("dbgscr")])
                    dsc = M("sv")
                    A.op("gp", lambda e, tb=tb: e.dma_start(
                        out=DBG["var"][4 + tb:5 + tb, :], in_=DBG["scr"]),
                        sem="sdb", n=16, waits=[dsc])
                    mbst["dbgscr"] = M("sdb")
                bank_last[bsum] = vr
                bank_last[bsq] = vr
                A.op("act", lambda e: e.activation(scrB, scrA, AF.Ln,
                                                   bias=c_eps[0:1, 0:1]),
                     sem="sa", waits=[vr, mbst.get("cconst")])
                A.op("act", lambda e: e.activation(scrA, scrB, AF.Exp,
                                                   scale=c_mhalf[0:1, 0:1]),
                     sem="sa")
                rstd_m = M("sa")
                A.op("dve", lambda e: e.tensor_copy(row_bf[0:1, :], scrA),
                     sem="sv", waits=[rstd_m, mbst.get("rowbf_guard")])
                rb_m = M("sv")
                A.op("pe", lambda e: e.matmul(bank(4), onesrz, row_bf,
                                              start=True, stop=True), sem="sp",
                     waits=[rb_m] + bank_waits(4))
                bc = M("sp")
                mbst["rowbf_guard"] = bc
                A.op("act", lambda e, sl=sl: e.activation(rstd_b[:, sl], bank(4),
                                                          AF.Copy), sem="sa",
                     waits=[bc])
                bank_last[4] = M("sa")
                mbst[f"rstd_{tag}_{tb}"] = bank_last[4]
            for b, mk in grp_marks:
                bank_last[b] = mk
        A.op("dve", lambda e: e.engine_nop(), sem="sv")
        last = M("sv")
        for b in range(4):
            bank_last[b] = last
        mbst[f"stats_{tag}"] = last

    def bcast_mb(tag, m_b):
        for tb in range(NTB):
            sl = slice(tb * 512, (tb + 1) * 512)
            A.op("pe", lambda e, sl=sl: e.matmul(
                bank(4), onesrz, m_row[:, sl], start=True, stop=True),
                sem="sp", waits=[mbst[f"mrow_{tag}_{tb}"]] + bank_waits(4))
            bc = M("sp")
            A.op("act", lambda e, sl=sl: e.activation(m_b[:, sl], bank(4), AF.Copy),
                 sem="sa", waits=[bc])
            bank_last[4] = M("sa")
        mbst[f"mb_{tag}"] = bank_last[4]

    def ffn_pass(tag, bufs, final):
        m_b, hbuf, wsl, xn_tt, utmp, stage, cs_row, hs_hi, hs_lo, xn32 = bufs
        for tt in range(NTB):
            sl = slice(tt * 512, (tt + 1) * 512)
            stg_w = [mbst[f"rstd_{tag}_{tt}"], mbst[f"mb_{tag}"],
                     slot_last.get("xn")]
            stg_m = {}
            for dc in range(NC):
                xs = xn_tt[:, dc * 512:(dc + 1) * 512]
                A.op("dve", lambda e, dc=dc, sl=sl: e.tensor_sub(
                    xn32, xbuf[:, dc, sl], m_b[:, sl]), sem="sv",
                    waits=stg_w if dc == 0 else [])
                A.op("dve", lambda e, xs=xs, sl=sl: e.tensor_mul(
                    xs, xn32, rstd_b[:, sl]), sem="sv")
                stg_m[dc] = M("sv")
            xn_ready = stg_m[NC - 1]
            if DEBUG_TAPS and tag == "ln1" and tt == 0 and not final:
                A.op("gp", lambda e: e.dma_start(out=DBG["xn"], in_=xn_tt),
                     sem="sdt", n=16, waits=[xn_ready])
            for dc in range(NC):
                A.op("pe", lambda e, dc=dc: e.matmul(
                    pAll[0:1, 6 * 512:7 * 512], ones128[:, 0:1],
                    xn_tt[:, dc * 512:(dc + 1) * 512],
                    start=(dc == 0), stop=(dc == NC - 1)),
                    sem="sp" if dc == NC - 1 else None,
                    waits=[stg_m[dc]] + (bank_waits(6) if dc == 0 else []))
            csm = M("sp")
            A.op("dve", lambda e: e.tensor_copy(cs_row[0:1, :],
                                                pAll[0:1, 6 * 512:7 * 512]),
                 sem="sv", waits=[csm])
            bank_last[6] = M("sv")
            cs_ready = bank_last[6]
            hm = None
            for pr in range(NPR):
                gslot, gname = wsl[pr % 2], f"wsl{pr % 2}"
                uslot, uname = wsl[2 + pr % 2], f"wsl{2 + pr % 2}"
                gsem = f"sw{pr % 2}"
                usem = f"sw{2 + pr % 2}"
                A.op("gp", lambda e, pr=pr, gslot=gslot: e.dma_start(
                    out=gslot.rearrange("p (c h) -> p c h", c=16),
                    in_=dt_wg[pr]),
                    sem=gsem, n=16, waits=[slot_last.get(gname)])
                gdma = M(gsem)
                A.op("gp", lambda e, pr=pr, uslot=uslot: e.dma_start(
                    out=uslot.rearrange("p (c h) -> p c h", c=16),
                    in_=dt_wu[pr]),
                    sem=usem, n=16, waits=[slot_last.get(uname)])
                udma = M(usem)
                for k in range(2):
                    hc = 2 * pr + k
                    gb, ub = hc % 2, 2 + hc % 2
                    hsl = slice(hc * 512, (hc + 1) * 512)
                    for c in range(NC):
                        A.op("pe", lambda e, c=c, k=k, gb=gb, gslot=gslot: e.matmul(
                            bank(gb),
                            gslot[:, c * 256 + k * 128: c * 256 + k * 128 + 128],
                            xn_tt[:, c * 512:(c + 1) * 512],
                            start=(c == 0), stop=False),
                            waits=([gdma, stg_m[c]] + bank_waits(gb)) if c == 0 else [])
                    A.op("pe", lambda e, gb=gb: e.matmul(
                        bank(gb), negrz, cs_row, start=False,
                        stop=True), sem="sp", waits=[cs_ready])
                    gmm = M("sp")
                    if k == 1:
                        slot_last[gname] = gmm
                    A.op("act", lambda e, hsl=hsl, gb=gb, hc=hc: e.activation(
                        hbuf[:, hsl], bank(gb), AF.Silu, scale=sgs[:, hc:hc + 1]),
                        sem="sa", waits=[gmm])
                    silu_m = M("sa")
                    bank_last[gb] = silu_m
                    for c in range(NC):
                        A.op("pe", lambda e, c=c, k=k, ub=ub, uslot=uslot: e.matmul(
                            bank(ub),
                            uslot[:, c * 256 + k * 128: c * 256 + k * 128 + 128],
                            xn_tt[:, c * 512:(c + 1) * 512],
                            start=(c == 0), stop=False),
                            waits=([udma] + bank_waits(ub)) if c == 0 else [])
                    A.op("pe", lambda e, ub=ub: e.matmul(
                        bank(ub), negrz, cs_row, start=False,
                        stop=True), sem="sp")
                    umm = M("sp")
                    if k == 1:
                        slot_last[uname] = umm
                    if hc == NHC - 1:
                        slot_last["xn"] = umm
                    A.op("act", lambda e, hc=hc, ub=ub: e.activation(
                        utmp[hc % 2], bank(ub), AF.Copy, scale=sus[:, hc:hc + 1]),
                        sem="sa", waits=[umm, slot_last.get(f"utmp{hc % 2}")])
                    up_m = M("sa")
                    bank_last[ub] = up_m
                    A.op("dve", lambda e, hsl=hsl, hc=hc: e.tensor_mul(
                        hbuf[:, hsl], hbuf[:, hsl], utmp[hc % 2]), sem="sv",
                        waits=[silu_m, up_m])
                    hm = M("sv")
                    slot_last[f"utmp{hc % 2}"] = hm
            if DEBUG_TAPS and tag == "ln1" and tt == 0 and not final:
                A.op("gp", lambda e: e.dma_start(out=DBG["hb"], in_=hbuf),
                     sem="sdt", n=16, waits=[hm])
                A.op("gp", lambda e: e.dma_start(out=DBG["mr"], in_=m_row),
                     sem="sdt", n=16)
                A.op("gp", lambda e: e.dma_start(out=DBG["rs"], in_=rstd_b[0:1, :]),
                     sem="sdt", n=16)
                A.op("gp", lambda e: e.dma_start(out=DBG["cs"], in_=cs_row[0:1, :]),
                     sem="sdt", n=16)
            for hc in range(NHC):
                A.op("pe", lambda e, hc=hc: e.matmul(
                    pAll[0:1, 6 * 512:7 * 512], ones128[:, 0:1],
                    hbuf[:, hc * 512:(hc + 1) * 512],
                    start=(hc == 0), stop=(hc == NHC - 1)),
                    sem="sp" if hc == NHC - 1 else None,
                    waits=([hm] + bank_waits(6)) if hc == 0 else [])
            hsm = M("sp")
            A.op("dve", lambda e: e.tensor_copy(hs_hi[0:1, :],
                                                pAll[0:1, 6 * 512:7 * 512]),
                 sem="sv", waits=[hsm])
            A.op("dve", lambda e: e.tensor_sub(hs_lo[0:1, :],
                                               pAll[0:1, 6 * 512:7 * 512],
                                               hs_hi[0:1, :]), sem="sv")
            bank_last[6] = M("sv")
            hs_ready = bank_last[6]
            for grp in ([0, 1, 2, 3, 4, 5, 6], [7, 8, 9, 10, 11, 12, 13], [14, 15]):
                for pr in range(NPR):
                    dslot, dname = wsl[pr % 4], f"wsl{pr % 4}"
                    dsem = f"sw{pr % 4}"
                    A.op("gp", lambda e, pr=pr, dslot=dslot: e.dma_start(
                        out=dslot.rearrange("p (a d) -> p a d", a=2),
                        in_=dt_wd[pr]),
                        sem=dsem, n=16, waits=[slot_last.get(dname)])
                    ddma = M(dsem)
                    for k in range(2):
                        hc = 2 * pr + k
                        for bi, dc in enumerate(grp):
                            w = []
                            if hc == 0:
                                w += [hs_ready] + bank_waits(bi)
                            if k == 0 and bi == 0:
                                w += [ddma]
                            is_last = (k == 1 and bi == len(grp) - 1)
                            A.op("pe", lambda e, k=k, dc=dc, bi=bi, dslot=dslot,
                                 hc=hc: e.matmul(
                                     bank(bi),
                                     dslot[:, k * 2048 + dc * 128:
                                           k * 2048 + dc * 128 + 128],
                                     hbuf[:, hc * 512:(hc + 1) * 512],
                                     start=(hc == 0), stop=False),
                                 sem="sp" if is_last else None, waits=w)
                            if is_last:
                                slot_last[dname] = M("sp")
                for bi, dc in enumerate(grp):
                    A.op("pe", lambda e, bi=bi: e.matmul(
                        bank(bi), negrz, hs_hi, start=False,
                        stop=False))
                    A.op("pe", lambda e, bi=bi: e.matmul(
                        bank(bi), negrz, hs_lo, start=False,
                        stop=True), sem="sp")
                    dmm = M("sp")
                    if not final:
                        A.op("dve", lambda e, dc=dc, bi=bi, sl=sl:
                             e.scalar_tensor_tensor(
                                 xbuf[:, dc, sl], bank(bi), sdh[:, dc:dc + 1],
                                 xbuf[:, dc, sl], ALU.mult, ALU.add),
                             sem="sv", waits=[dmm])
                        bank_last[bi] = M("sv")
                        if tt == NTB - 1:
                            mbst[f"evlast_{dc}"] = bank_last[bi]
                    else:
                        st = stage[dc % 6]
                        A.op("dve", lambda e, bi=bi, sl=sl, st=st, dc=dc:
                             e.scalar_tensor_tensor(
                                 st, bank(bi), sdh[:, dc:dc + 1],
                                 xbuf[:, dc, sl], ALU.mult, ALU.add),
                             sem="sv", waits=[dmm, slot_last.get(f"stage{dc % 6}")])
                        ev = M("sv")
                        bank_last[bi] = ev
                        ssem = f"so{dc % 6}"
                        A.op("sy", lambda e, dc=dc, tt=tt, st=st: e.dma_start(
                            out=dt_out[dc, :, tt, :], in_=st), sem=ssem, n=16,
                            waits=[ev])
                        slot_last[f"stage{dc % 6}"] = M(ssem)
        A.op("dve", lambda e: e.engine_nop(), sem="sv")
        mbst[f"ffn_{tag}_done"] = M("sv")

    ffn_es = ExitStack()
    relb = ffn_es.enter_context(nc.sbuf_tensor("relb_sb", [32, 32], BF16)).ap()
    onehot = ffn_es.enter_context(nc.sbuf_tensor("onehot", [32, 512], BF16)).ap()
    twin = ffn_es.enter_context(nc.sbuf_tensor("twin", [32, 512], BF16)).ap()
    m_b = ffn_es.enter_context(nc.sbuf_tensor("m_b", [128, S], BF16)).ap()
    hbuf = ffn_es.enter_context(nc.sbuf_tensor("hbuf", [128, NHC * 512], BF16)).ap()
    wsl = [ffn_es.enter_context(nc.sbuf_tensor(f"wsl{i}", [128, 4096], BF16)).ap()
           for i in range(4)]
    xn_tt = ffn_es.enter_context(nc.sbuf_tensor("xn_tt", [128, 8192], BF16)).ap()
    utmp = [ffn_es.enter_context(nc.sbuf_tensor(f"utmp{i}", [128, 512], BF16)).ap()
            for i in range(2)]
    sq0 = ffn_es.enter_context(nc.sbuf_tensor("sq0", [128, S // 2], BF16)).ap()
    sq1 = ffn_es.enter_context(nc.sbuf_tensor("sq1", [128, S // 2], BF16)).ap()
    stage = [ffn_es.enter_context(nc.sbuf_tensor(f"stage{i}", [128, 512], F32)).ap()
             for i in range(6)]
    cs_row = ffn_es.enter_context(nc.sbuf_tensor("cs_row", [128, 512], BF16)).ap()
    if DEBUG_TAPS:
        DBG["scr"] = ffn_es.enter_context(nc.sbuf_tensor("dbg_scr", [1, 512], F32)).ap()
    hs_hi = ffn_es.enter_context(nc.sbuf_tensor("hs_hi", [128, 512], BF16)).ap()
    hs_lo = ffn_es.enter_context(nc.sbuf_tensor("hs_lo", [128, 512], BF16)).ap()
    xn32 = ffn_es.enter_context(nc.sbuf_tensor("xn32", [128, 512], F32)).ap()
    ffn_bufs = (m_b, hbuf, wsl, xn_tt, utmp, stage, cs_row, hs_hi, hs_lo, xn32)

    A.new_block()
    for sb, dr in ((idb, dt_idb), (ones128, dt_ones), (onesr, dt_onesr),
                   (negr, dt_neg), (sgs, dt_sg), (sus, dt_su), (sdh, dt_sd),
                   (onehot, dt_oh)):
        A.op("sy", lambda e, sb=sb, dr=dr: e.dma_start(out=sb, in_=dr),
             sem="sc", n=16)
    consts_done = M("sc")
    A.op("dve", lambda e: e.tensor_scalar(sdh, sdh, 0.5, None, ALU.mult),
         sem="sv", waits=[consts_done])
    A.op("dve", lambda e: e.memset(c_eps, 1e-5), sem="sv")
    A.op("dve", lambda e: e.memset(c_mhalf, -0.5), sem="sv")
    A.op("dve", lambda e: e.memset(negrz, 0.0), sem="sv")
    A.op("dve", lambda e: e.memset(onesrz, 0.0), sem="sv")
    for zb in (m_row, row_bf, r31row, rrow_bf, srow):
        A.op("dve", lambda e, zb=zb: e.memset(zb, 0.0), sem="sv")
    A.op("dve", lambda e: e.tensor_copy(negrz[0:1, :], negr), sem="sv",
         waits=[consts_done])
    A.op("dve", lambda e: e.tensor_copy(onesrz[0:1, :], onesr), sem="sv")
    mbst["cconst"] = M("sv")
    x_marks = {}
    for dc in range(NC):
        A.op("gp", lambda e, dc=dc: e.dma_start(out=xbuf[:, dc, :], in_=dt_x[dc]),
             sem="sx", n=16)
        x_marks[dc] = M("sx")
    x_loaded = x_marks[NC - 1]
    A.op("gp", lambda e: e.dma_start(out=relb, in_=dt_rb), sem="srb", n=16)
    rb_done = M("srb")

    A.op("pe", lambda e: e.matmul(pAll[0:32, 5 * 512:6 * 512], relb, onehot,
                                  start=True, stop=True),
         sem="sp", waits=[consts_done, rb_done])
    tprime = M("sp")
    A.op("act", lambda e: e.activation(twin, pAll[0:32, 5 * 512:6 * 512], AF.Copy),
         sem="sa", waits=[tprime])
    twin_done = M("sa")
    bank_last[5] = twin_done
    A.op("sy", lambda e: e.dma_start(out=d_toep[:, 0:384], in_=twin[0:32, 0:384]),
         sem="stw", n=16, waits=[twin_done])
    mbst["toepw"] = M("stw")
    A.op("pe", lambda e: e.transpose(pTrb[0:1, 0:32], twin[0:32, 240:241],
                                     idb[0:32, 0:32]), sem="sp",
         waits=[twin_done, consts_done])
    trp = M("sp")
    A.op("dve", lambda e: e.tensor_copy(r31row[0:1, :], pTrb[0:1, 0:32]),
         sem="sv", waits=[trp, mbst["cconst"]])
    bank_last[7] = M("sv")
    mbst["r31"] = bank_last[7]

    stats_pass("ln1", (sq0, sq1), src_ready=x_marks)
    bcast_mb("ln1", m_b)

    with nc.Block("b0") as blk:
        A.emit(blk)

    A.new_block()
    for buf_ in (cs_row, hs_hi, hs_lo):
        A.op("dve", lambda e, buf_=buf_: e.memset(buf_, 0.0), sem="sv")
    ffn_pass("ln1", ffn_bufs, final=False)
    stats_pass("ln2", (sq0, sq1),
               src_ready={dc: mbst[f"evlast_{dc}"] for dc in range(NC)})
    with nc.Block("b1") as blk:
        A.emit(blk)

    ffn_es.close()
    at_es = ExitStack()
    toep = at_es.enter_context(nc.sbuf_tensor("toep", [128, 2, 4, 256], BF16)).ap()
    wqkv = [at_es.enter_context(nc.sbuf_tensor(f"wqkv{i}", [128, 4096], BF16)).ap()
            for i in range(2)]
    Qsg = at_es.enter_context(nc.sbuf_tensor("Qsg", [128, 2, S], BF16)).ap()
    Ksg = at_es.enter_context(nc.sbuf_tensor("Ksg", [128, 2, S], BF16)).ap()
    Vaug = at_es.enter_context(nc.sbuf_tensor("Vaug", [128, 16, 260], BF16)).ap()
    Ebuf = [at_es.enter_context(nc.sbuf_tensor(f"Ebuf{i}", [128, 1024], BF16)).ap()
            for i in range(2)]
    rb_sb = at_es.enter_context(nc.sbuf_tensor("rb_sb", [128, 1024], BF16)).ap()
    Ksub = at_es.enter_context(nc.sbuf_tensor("Ksub", [128, 128], BF16)).ap()
    Qsub = at_es.enter_context(nc.sbuf_tensor("Qsub", [128, 128], BF16)).ap()
    ncsw = at_es.enter_context(nc.sbuf_tensor("ncsw", [128, 256], BF16)).ap()
    abuf = at_es.enter_context(nc.sbuf_tensor("abuf", [128, NC, S], BF16)).ap()
    Kpad = at_es.enter_context(nc.sbuf_tensor("Kpad", [128, 2, S], BF16)).ap()

    A.new_block()
    A.op("dve", lambda e: e.memset(Kpad, 0.0), sem="sv")
    A.op("dve", lambda e: e.memset(ncsw, 0.0), sem="sv")
    if DEBUG_TAPS:
        for dc in range(NC):
            A.op("gp", lambda e, dc=dc: e.dma_start(out=dt_tap1[dc],
                                                    in_=xbuf[:, dc, :]),
                 sem="sdt", n=16)
        tap1_done = M("sdt")
    def emit_toep(s):
        sl = s % 2
        for p in range(128):
            w = ([mbst.get(f"toepu{sl}"),
                  mbst["toepw"] if s < 2 else None] if p < 2 else [])
            A.op("sy", lambda e, p=p, s=s, sl=sl: e.dma_start(
                out=toep[p:p + 1, sl, :, :],
                in_=d_toep[s * 4:(s + 1) * 4,
                           127 - p:127 - p + 256].unsqueeze(0)),
                sem="stp", n=16, waits=w)
        mbst[f"toep_{s}"] = M("stp")

    emit_toep(0)
    emit_toep(1)

    for c in range(NC):
        A.op("pe", lambda e, c=c: e.transpose(
            pTrb[:, 0:128], rstd_b[:, c * 128:(c + 1) * 128], idb), sem="sp",
            waits=bank_waits(7))
        tm = M("sp")
        A.op("dve", lambda e, c=c: e.tensor_copy(rstd_col[:, c:c + 1],
                                                 pTrb[:, 0:1]), sem="sv",
             waits=[tm])
        bank_last[7] = M("sv")
    rstdcol_done = bank_last[7]

    exp_last = {}
    aev = None

    def emit_kpad(h_t):
        r0t = (h_t % 2) * 64
        A.op("dve", lambda e, h_t=h_t, r0t=r0t: e.tensor_copy(
            Kpad[r0t:r0t + 64, h_t % 2, :],
            Ksg[r0t:r0t + 64, h_t // 2, :]), sem="sv",
            waits=[mbst.get(f"smm_par{h_t % 2}")])
        mbst[f"kpad{h_t % 2}"] = M("sv")

    for sg in range(8):
        def emit_wdma(i, dt_w):
            j = 3 * sg + i
            buf = wqkv[j % 2]
            qsem = f"swq{j % 2}"
            A.op("gp", lambda e, dt_w=dt_w, sg=sg, buf=buf: e.dma_start(
                out=buf.rearrange("p (c h) -> p c h", c=16),
                in_=dt_w[sg]),
                sem=qsem, n=16,
                waits=[slot_last.get(f"wqkv{j % 2}"),
                       mbst.get("pass1_done") if (i == 0 and EXP_DELAY) else None])
            return M(qsem)

        wdma = [emit_wdma(0, dt_wq), emit_wdma(1, dt_wk)]
        projm = {}
        def make_p1_stages():
            stages = []
            state = {}

            def s1(h):
                oc = h // 2
                r0 = (h % 2) * 64
                A.op("dve", lambda e, oc=oc, r0=r0: e.tensor_copy(
                    Ksub[r0:r0 + 64, :], Ksg[r0:r0 + 64, oc, 0:S:SUBS]),
                    sem="sv", waits=[projm["k"], mbst.get("ksub_guard"),
                                     mbst.get("dbgqkv") if DEBUG_TAPS else None])
                A.op("dve", lambda e, oc=oc, r0=r0: e.tensor_copy(
                    Qsub[r0:r0 + 64, :], Qsg[r0:r0 + 64, oc, 0:S:SUBS]),
                    sem="sv", waits=[projm["q"]])
                state["sub"] = M("sv")

            def s2(h):
                r0 = (h % 2) * 64
                A.op("pe", lambda e, r0=r0: e.matmul(
                    pAll[:, 0:128], Qsub[r0:r0 + 64, :], Ksub[r0:r0 + 64, :],
                    start=True, stop=True), sem="sp",
                    waits=[state["sub"]] + bank_waits(0))
                p1 = M("sp")
                mbst["ksub_guard"] = p1
                A.op("dve", lambda e: e.tensor_reduce(
                    mx1, pAll[:, 0:128], AX.X, ALU.max), sem="sv", waits=[p1])
                state["mx"] = M("sv")
                bank_last[0] = state["mx"]

            def s3(h):
                A.op("pe", lambda e: e.transpose(pTrb[0:1, 0:128], mx1, idb),
                     sem="sp", waits=[state["mx"]] + bank_waits(7))
                trm = M("sp")
                A.op("dve", lambda e: e.tensor_reduce(
                    srow[0:1, 0:1], pTrb[0:1, 0:128], AX.X, ALU.max),
                    sem="sv", waits=[trm])
                bank_last[7] = M("sv")
                state["srow"] = bank_last[7]

            def s4(h):
                par = h
                gh = sg * 4 + h
                A.op("pe", lambda e: e.matmul(pAll[:, 0:1], onesrz,
                                              srow[:, 0:1], start=True,
                                              stop=True), sem="sp",
                     waits=[state["srow"]] + bank_waits(0))
                bmm = M("sp")
                A.op("dve", lambda e, par=par: e.tensor_scalar(
                    bm[par], pAll[:, 0:1], -1.0, -MARGIN, ALU.mult, ALU.add),
                    sem="sv", waits=[bmm, exp_last.get(par)])
                bm_m = M("sv")
                bank_last[0] = bm_m
                A.op("pe", lambda e, gh=gh: e.matmul(
                    pAll[:, 0:1], onesrz, r31row[:, gh:gh + 1],
                    start=True, stop=True), sem="sp",
                    waits=[mbst["r31"], bm_m] + bank_waits(0))
                r31m = M("sp")
                A.op("dve", lambda e, par=par: e.tensor_add(
                    bm31[par], pAll[:, 0:1], bm[par]), sem="sv", waits=[r31m])
                bm31_m = M("sv")
                bank_last[0] = bm31_m
                mbst[f"bm31m_{h}"] = bm31_m
                mbst["pass1_done"] = bm31_m

            for h in range(4):
                for s in (s1, s2, s3, s4):
                    stages.append(lambda s=s, h=h: s(h))
            return stages

        mbst["p1_stages"] = make_p1_stages()
        for i, name in enumerate(("q", "k", "v")):
            buf = wqkv[(3 * sg + i) % 2]
            for c in range(NC):
                A.op("pe", lambda e, c=c, buf=buf: e.matmul(
                    pAll[0:1, 0:256], ones128[:, 0:1],
                    buf[:, c * 256:(c + 1) * 256],
                    start=(c == 0), stop=(c == NC - 1)),
                    sem="sp" if c == NC - 1 else None,
                    waits=([wdma[i]] + bank_waits(0)) if c == 0 else [])
            cwm = M("sp")
            A.op("dve", lambda e: e.tensor_scalar(ncsw[0:1, :],
                                                  pAll[0:1, 0:256], -1.0,
                                                  None, ALU.mult), sem="sv",
                 waits=[cwm])
            bank_last[0] = M("sv")
            ncsw_ready = bank_last[0]
            if name in ("q", "k"):
                dst = Qsg if name == "q" else Ksg
                pm_last = None
                for oc in range(2):
                    for tb in range(NTB):
                        b = 1 + (oc * NTB + tb) % 2
                        for c in range(NC):
                            A.op("pe", lambda e, c=c, oc=oc, tb=tb, b=b, buf=buf:
                                 e.matmul(
                                     bank(b),
                                     buf[:, c * 256 + oc * 128:
                                         c * 256 + oc * 128 + 128],
                                     xbuf[:, c, tb * 512:(tb + 1) * 512],
                                     start=(c == 0), stop=False),
                                 waits=bank_waits(b) if c == 0 else [])
                        A.op("pe", lambda e, oc=oc, tb=tb, b=b: e.matmul(
                            bank(b), ncsw[:, oc * 128:oc * 128 + 128],
                            m_row[:, tb * 512:(tb + 1) * 512],
                            start=False, stop=True), sem="sp",
                            waits=[ncsw_ready, mbst[f"mrow_ln2_{tb}"]])
                        pm = M("sp")
                        A.op("dve", lambda e, dst=dst, oc=oc, tb=tb, b=b:
                             e.tensor_mul(
                                 dst[:, oc, tb * 512:(tb + 1) * 512], bank(b),
                                 rstd_b[:, tb * 512:(tb + 1) * 512]),
                             sem="sv", waits=[pm, mbst[f"rstd_ln2_{tb}"]])
                        bank_last[b] = M("sv")
                        pm_last = bank_last[b]
                projm[name] = pm_last
                if name == "q":
                    slot_last[f"wqkv{(3 * sg) % 2}"] = projm["q"]
                    wdma.append(emit_wdma(2, dt_wv))
                else:
                    slot_last[f"wqkv{(3 * sg + 1) % 2}"] = projm["k"]
                    emit_kpad(0)
                    emit_kpad(1)
            else:
                A.op("dve", lambda e: e.memset(Vaug, 1.0), sem="sv",
                     waits=[slot_last.get("Vaug"), mbst.get("last_pv")])
                vset = M("sv")
                vm_last = None
                p1_stages = mbst.pop("p1_stages", [])
                for ts in range(16):
                    vb = 3 + ts % 2
                    for c in range(NC):
                        A.op("pe", lambda e, c=c, ts=ts, buf=buf, vb=vb: e.matmul(
                            pAll[:, vb * 512: vb * 512 + 256],
                            xbuf[:, c, ts * 128:(ts + 1) * 128],
                            buf[:, c * 256:(c + 1) * 256],
                            start=(c == 0), stop=False),
                            waits=([vset] + bank_waits(vb)) if c == 0
                            else (bank_waits(vb) if c == 0 else [])) if c == 0 else                         A.op("pe", lambda e, c=c, ts=ts, buf=buf, vb=vb: e.matmul(
                            pAll[:, vb * 512: vb * 512 + 256],
                            xbuf[:, c, ts * 128:(ts + 1) * 128],
                            buf[:, c * 256:(c + 1) * 256],
                            start=(c == 0), stop=False))
                    A.op("pe", lambda e, ts=ts, vb=vb: e.matmul(
                        pAll[:, vb * 512: vb * 512 + 256],
                        m_row[:, ts * 128:(ts + 1) * 128],
                        ncsw[:, 0:256],
                        start=False, stop=True), sem="sp", waits=[ncsw_ready])
                    vm = M("sp")
                    for hh in range(4):
                        A.op("act", lambda e, ts=ts, hh=hh, vb=vb: e.activation(
                            Vaug[:, ts, hh * 65: hh * 65 + 64],
                            pAll[:, vb * 512 + hh * 64: vb * 512 + hh * 64 + 64],
                            AF.Copy, scale=rstd_col[:, ts:ts + 1]), sem="sa",
                            waits=[vm, rstdcol_done] if hh == 0 else [])
                    bank_last[vb] = M("sa")
                    vm_last = bank_last[vb]
                    if ts < len(p1_stages):
                        p1_stages[ts]()
                for s in p1_stages[16:]:
                    s()
                projm["v"] = vm_last
        if DEBUG_TAPS and sg == 0:
            A.op("gp", lambda e: e.dma_start(out=DBG["q"], in_=Qsg), sem="sdt",
                 n=16, waits=[projm["q"]])
            A.op("gp", lambda e: e.dma_start(out=DBG["k"], in_=Ksg), sem="sdt",
                 n=16, waits=[projm["k"]])
            A.op("gp", lambda e: e.dma_start(out=DBG["va"], in_=Vaug), sem="sdt",
                 n=16, waits=[projm["v"]])
            mbst["dbgqkv"] = M("sdt")
        slot_last[f"wqkv{(3 * sg + 2) % 2}"] = projm["v"]


        for h in range(4):
            gh = sg * 4 + h
            oc = h // 2
            r0 = (h % 2) * 64
            par = h
            bm31_m = mbst[f"bm31m_{h}"]
            for qh in range(2):
                pv_last = None
                pend = []

                def emit_pv(c, eb, em, h=h):
                    nonlocal pv_last
                    for qb in range(2):
                        w = [em]
                        if c == 0:
                            w += [projm["v"]] + bank_waits(5 + qb)
                        is_l = (c == 15)
                        A.op("pe", lambda e, c=c, qb=qb, h=h, eb=eb,
                             is_l=is_l: e.matmul(
                            pAll[0:65, (5 + qb) * 512:(5 + qb) * 512 + 512],
                            Vaug[:, c, h * 65: h * 65 + 65],
                            eb[:, qb * 512:(qb + 1) * 512],
                            start=(c == 0), stop=is_l),
                            sem="sp" if qb == 1 else None,
                            waits=w)
                    pv_m = M("sp")
                    slot_last[f"E{c % 2}"] = pv_m
                    if c == 15:
                        pv_last = pv_m

                def make_norm_stages(pv_last, gh=gh, r0=r0, qh=qh):
                    st = {}

                    def stage_a():
                        A.op("dve", lambda e: e.reciprocal(
                            scrA, pAll[64:65, 5 * 512: 5 * 512 + 512]),
                            sem="sv", waits=[pv_last, mbst.get("rrow_guard")])
                        A.op("dve", lambda e: e.tensor_copy(
                            rrow_bf[0:1, 0:512], scrA), sem="sv")
                        A.op("dve", lambda e: e.reciprocal(
                            scrA, pAll[64:65, 6 * 512: 6 * 512 + 512]), sem="sv")
                        A.op("dve", lambda e: e.tensor_copy(
                            rrow_bf[0:1, 512:1024], scrA), sem="sv")
                        st["rm"] = M("sv")

                    def stage_b():
                        A.op("pe", lambda e: e.matmul(
                            bank(0), onesrz, rrow_bf[:, 0:512],
                            start=True, stop=True),
                             sem="sp", waits=[st["rm"]] + bank_waits(0))
                        rbm = M("sp")
                        A.op("act", lambda e: e.activation(
                            rb_sb[:, 0:512], bank(0), AF.Copy),
                            sem="sa", waits=[rbm])
                        bank_last[0] = M("sa")
                        st["rb0"] = bank_last[0]

                    def stage_c():
                        A.op("pe", lambda e: e.matmul(
                            bank(0), onesrz, rrow_bf[:, 512:1024],
                            start=True, stop=True),
                             sem="sp", waits=[st["rm"]] + bank_waits(0))
                        rbm = M("sp")
                        mbst["rrow_guard"] = rbm
                        A.op("act", lambda e: e.activation(
                            rb_sb[:, 512:1024], bank(0), AF.Copy),
                            sem="sa", waits=[rbm])
                        bank_last[0] = M("sa")
                        st["rb1"] = bank_last[0]

                    def stage_d():
                        fc = gh // 2
                        A.op("dve", lambda e, fc=fc, r0=r0, qh=qh: e.tensor_mul(
                            abuf[r0:r0 + 64, fc, qh * 1024: qh * 1024 + 1024],
                            pAll[0:64, 5 * 512: 5 * 512 + 1024],
                            rb_sb[0:64, 0:1024]), sem="sv",
                            waits=[st["rb0"], st["rb1"], pv_last])
                        ev = M("sv")
                        bank_last[5] = ev
                        bank_last[6] = ev
                        mbst["aev"] = ev

                    return [stage_a, stage_b, stage_c, stage_d]

                for c in range(16):
                    sb = 1 + (c % 2) * 2
                    eb = Ebuf[c % 2]
                    pn = mbst.get("pending_norm")
                    if pn:
                        pn.pop(0)()
                        if not pn:
                            mbst.pop("pending_norm")
                            aev = mbst["aev"]
                    if len(pend) >= 2:
                        if pend[0][0] == 0:
                            pn = mbst.get("pending_norm")
                            while pn:
                                pn.pop(0)()
                            if "pending_norm" in mbst:
                                mbst.pop("pending_norm")
                                aev = mbst["aev"]
                        emit_pv(*pend.pop(0))
                    blocks = {0: [], 1: []}
                    for j in (0, 1):
                        qbl = c + j - qh * 8
                        if 0 <= qbl < 8:
                            blocks[qbl // 4].append((j, qbl % 4))
                    for qb in range(2):
                        bia = blocks[qb]
                        kw = ([mbst[f"kpad{h % 2}"]]
                              if (qh == 0 and c == 0 and qb == 0) else [])
                        A.op("pe", lambda e, c=c, qb=qb, sb=sb, qh=qh, oc=oc,
                             hp=h % 2, stp=(len(bia) == 0): e.matmul(
                                 bank(sb + qb),
                                 Kpad[:, hp, c * 128:(c + 1) * 128],
                                 Qsg[:, oc,
                                     qh * 1024 + qb * 512:
                                     qh * 1024 + qb * 512 + 512],
                                 start=True, stop=stp),
                            sem="sp" if (len(bia) == 0 and qb == 1) else None,
                            waits=(bank_waits(sb + qb) + kw) if c < 2 else kw)
                        for bidx, (j, qp) in enumerate(bia):
                            lastb = (bidx == len(bia) - 1)
                            A.op("pe", lambda e, j=j, qp=qp, sb=sb, qb=qb,
                                 sg=sg, h=h, lastb=lastb: e.matmul(
                                     pAll[:, (sb + qb) * 512 + qp * 128:
                                          (sb + qb) * 512 + qp * 128 + 128],
                                     idb, toep[:, sg % 2, h,
                                               128 * j:128 * j + 128],
                                     start=False, stop=lastb),
                                sem="sp" if (lastb and qb == 1) else None,
                                waits=[mbst[f"toep_{sg}"]]
                                if (h == 0 and qh == 0 and c == 0) else [])
                    smm = M("sp")
                    split = max(0, min(1024, (c + 2 - qh * 8) * 128))
                    ew = [smm, bm31_m, slot_last.get(f"E{c % 2}")]
                    if split > 0:
                        A.op("act", lambda e, sb=sb, eb=eb, split=split,
                             par=par: e.activation(
                                 eb[:, 0:split],
                                 pAll[:, sb * 512: sb * 512 + split],
                                 AF.Exp, bias=bm[par][:, 0:1]), sem="sa",
                            waits=ew)
                    if split < 1024:
                        A.op("act", lambda e, sb=sb, eb=eb, split=split,
                             par=par: e.activation(
                                 eb[:, split:1024],
                                 pAll[:, sb * 512 + split: sb * 512 + 1024],
                                 AF.Exp, bias=bm31[par][:, 0:1]), sem="sa",
                            waits=ew if split == 0 else [])
                    em = M("sa")
                    if gh == sg * 4 + 3 and qh == 0 and c == 0:
                        mbst["exp_probe"] = em
                    if DEBUG_TAPS and gh == 0 and qh == 0 and c == 15:
                        A.op("gp", lambda e, eb=eb: e.dma_start(out=DBG["e0"], in_=eb),
                             sem="sdt", n=16, waits=[em])
                        slot_last[f"E{c % 2}"] = M("sdt")
                        mbst["dbge0"] = slot_last[f"E{c % 2}"]
                    exp_last[par] = em
                    bank_last[sb] = em
                    bank_last[sb + 1] = em
                    pend.append((c, eb, em))
                while pend:
                    emit_pv(*pend.pop(0))
                mbst["pending_norm"] = make_norm_stages(pv_last)
                mbst["last_pv"] = pv_last
            mbst[f"smm_par{h % 2}"] = smm
            if h < 2:
                emit_kpad(h + 2)

        mbst[f"toepu{sg % 2}"] = smm
        if sg < 6:
            emit_toep(sg + 2)
        slot_last["Vaug"] = aev

    if mbst.get("pending_norm") is not None:
        for f in mbst.pop("pending_norm"):
            f()
        aev = mbst["aev"]
    if DEBUG_TAPS:
        A.op("gp", lambda e: e.dma_start(out=DBG["ab"], in_=abuf), sem="sdt",
             n=16, waits=[aev])
        mbst["dbgab"] = M("sdt")
    a_done = aev
    for grp in ([0, 1, 2, 3, 4, 5, 6], [7, 8, 9, 10, 11, 12, 13], [14, 15]):
        for tb in range(NTB):
            mm_m = {}
            for fc in range(NC):
                buf = wqkv[fc % 2]
                osem = f"swq{fc % 2}"
                A.op("gp", lambda e, fc=fc, g0=grp[0], gl=len(grp), buf=buf:
                     e.dma_start(
                         out=buf[:, 0:gl * 128],
                         in_=dt_wo[fc][:, g0 * 128: g0 * 128 + gl * 128]),
                     sem=osem, n=16, waits=[slot_last.get(f"wqkv{fc % 2}")])
                wom = M(osem)
                for bi in range(len(grp)):
                    w = [wom]
                    if fc == 0:
                        w += bank_waits(bi) + [a_done]
                        if DEBUG_TAPS:
                            w += [tap1_done]
                    is_l = (bi == len(grp) - 1)
                    A.op("pe", lambda e, fc=fc, bi=bi, tb=tb, buf=buf: e.matmul(
                        bank(bi), buf[:, bi * 128: bi * 128 + 128],
                        abuf[:, fc, tb * 512:(tb + 1) * 512],
                        start=(fc == 0), stop=(fc == NC - 1)),
                        sem="sp" if is_l else None,
                        waits=w)
                    if is_l:
                        mk = M("sp")
                        slot_last[f"wqkv{fc % 2}"] = mk
                        if fc == NC - 1:
                            mm_all = M("sp") if False else mk
            for bi, ocg in enumerate(grp):
                A.op("dve", lambda e, bi=bi, ocg=ocg, tb=tb: e.tensor_add(
                    xbuf[:, ocg, tb * 512:(tb + 1) * 512], bank(bi),
                    xbuf[:, ocg, tb * 512:(tb + 1) * 512]), sem="sv",
                    waits=[mm_all])
                bank_last[bi] = M("sv")
                if tb == NTB - 1:
                    mbst[f"oev_{ocg}"] = bank_last[bi]
    stats_pass("ln3", (Ebuf[0], Ebuf[1]),
               src_ready={dc: mbst[f"oev_{dc}"] for dc in range(NC)})
    with nc.Block("b2") as blk:
        A.emit(blk)

    at_es.close()
    f2 = ExitStack()
    m_b2 = f2.enter_context(nc.sbuf_tensor("m_b2", [128, S], BF16)).ap()
    hbuf2 = f2.enter_context(nc.sbuf_tensor("hbuf2", [128, NHC * 512], BF16)).ap()
    wsl2 = [f2.enter_context(nc.sbuf_tensor(f"w2sl{i}", [128, 4096], BF16)).ap()
            for i in range(4)]
    xn2 = f2.enter_context(nc.sbuf_tensor("xn2", [128, 8192], BF16)).ap()
    ut2 = [f2.enter_context(nc.sbuf_tensor(f"u2t{i}", [128, 512], BF16)).ap()
           for i in range(2)]
    st2 = [f2.enter_context(nc.sbuf_tensor(f"st2{i}", [128, 512], F32)).ap()
           for i in range(6)]
    cs2 = f2.enter_context(nc.sbuf_tensor("cs2", [128, 512], BF16)).ap()
    hh2 = f2.enter_context(nc.sbuf_tensor("hh2", [128, 512], BF16)).ap()
    hl2 = f2.enter_context(nc.sbuf_tensor("hl2", [128, 512], BF16)).ap()
    xn32b = f2.enter_context(nc.sbuf_tensor("xn32b", [128, 512], F32)).ap()

    A.new_block()
    if DEBUG_TAPS:
        for dc in range(NC):
            A.op("gp", lambda e, dc=dc: e.dma_start(out=dt_tap2[dc],
                                                    in_=xbuf[:, dc, :]),
                 sem="sdt", n=16)
    for k in ("wsl0", "wsl1", "wsl2", "wsl3", "utmp0", "utmp1", "xn",
              "stage0", "stage1", "stage2", "stage3", "stage4", "stage5", "stage6", "stage7", "Vaug", "wqkv0", "wqkv1",
              "E0", "E1"):
        slot_last.pop(k, None)
    for buf_ in (cs2, hh2, hl2):
        A.op("dve", lambda e, buf_=buf_: e.memset(buf_, 0.0), sem="sv")
    bcast_mb("ln3", m_b2)
    ffn_pass("ln3", (m_b2, hbuf2, wsl2, xn2, ut2, st2, cs2, hh2, hl2, xn32b), final=True)
    A.op("sy", lambda e: e.nop(),
         waits=[(f"so{i}", A.cnt[f"so{i}"]) for i in range(8)])
    with nc.Block("b3") as blk:
        A.emit(blk)

    f2.close()
    return nc


_CACHED = {}


def _host_prep(inputs):
    def lowbyte(a):
        a = np.ascontiguousarray(a.astype(np.int32, copy=False))
        return a.view(np.uint8)[:, 0::4]

    wg8 = lowbyte(inputs["L0_ffn_wq_gate"])
    wu8 = lowbyte(inputs["L0_ffn_wq_up"])
    wd8 = lowbyte(inputs["L0_ffn_wq_down"])

    G = np.ascontiguousarray(wg8.T)
    wg = np.ascontiguousarray(
        G.reshape(16, 128, NPR, 256).transpose(2, 1, 0, 3))
    U = np.ascontiguousarray(wu8.T)
    wu = np.ascontiguousarray(
        U.reshape(16, 128, NPR, 256).transpose(2, 1, 0, 3))
    Dw = np.ascontiguousarray(wd8.T)
    wd = np.ascontiguousarray(
        Dw.reshape(NPR, 2, 128, D).transpose(0, 2, 1, 3))

    def attw(W):
        return np.ascontiguousarray(
            W.reshape(16, 128, 8, 256).transpose(2, 1, 0, 3)).astype(np.float32)

    shared = {
        "wg": wg, "wu": wu, "wd": wd,
        "wqa": attw(inputs["L0_attn_Wq"]),
        "wka": attw(inputs["L0_attn_Wk"]),
        "wva": attw(inputs["L0_attn_Wv"]),
        "woa": np.ascontiguousarray(
            inputs["L0_attn_Wo"].reshape(16, 128, D)).astype(np.float32),
        "relb": np.ascontiguousarray(inputs["L0_attn_rel_bias"]).astype(np.float32),
        "sgsd": np.ascontiguousarray(
            inputs["L0_ffn_s_gate"].reshape(NHC, 128).T).astype(np.float32),
        "susd": np.ascontiguousarray(
            inputs["L0_ffn_s_up"].reshape(NHC, 128).T).astype(np.float32),
        "sdsd": np.ascontiguousarray(
            inputs["L0_ffn_s_down"].reshape(NC, 128).T).astype(np.float32),
        "idbd": np.eye(128, dtype=np.float32).astype(BFD),
        "ones128d": np.ones((128, 1), dtype=BFD),
        "onesrowd": np.ones((1, 128), dtype=BFD),
        "neg128rd": np.full((1, 128), -128.0, dtype=BFD),
        "onehotd": _host_onehot_win(),
    }
    return shared


def kernel(**inputs):
    x = inputs["L0_attn_hidden_states"]
    B = x.shape[0]
    shared = _host_prep(inputs)
    if "nc" not in _CACHED:
        _CACHED["nc"] = build_program()
    nc = _CACHED["nc"]
    in_maps = []
    for b in range(B):
        m = dict(shared)
        m["x"] = np.ascontiguousarray(x[b].T.reshape(NC, 128, S)).astype(np.float32)
        in_maps.append(m)
    res = run_bass_kernel_spmd(nc, in_maps, core_ids=list(range(B)))
    out = np.empty((B, S, D), dtype=np.float32)
    for b in range(B):
        o = res.results[b]["out"]
        out[b] = o.transpose(2, 3, 0, 1).reshape(S, D)
    return out


# version 35
# speedup vs baseline: 1.0586x; 1.0586x over previous
import math
import numpy as np
import ml_dtypes
from contextlib import ExitStack

import concourse.bass as bass
import concourse.mybir as mybir
from concourse.bass_utils import run_bass_kernel_spmd

F32 = mybir.dt.float32
BF16 = mybir.dt.bfloat16
U8 = mybir.dt.uint8
AF = mybir.ActivationFunctionType
ALU = mybir.AluOpType
AX = mybir.AxisListType

BFD = ml_dtypes.bfloat16

S = 2048
D = 2048
HID = 5632
NC = D // 128
NTB = S // 512
NHC = HID // 128
NPR = NHC // 2
MARGIN = 16.0
SUBS = 16

DEBUG_TAPS = False
EXP_DELAY = True
CONST_MHAT = False


def _bucket_of_delta(delta):
    NB, MD = 32, 128
    rp = np.maximum(delta, 0)
    mx = NB // 2
    small = rp < mx
    rpf = np.maximum(rp, 1).astype(np.float32)
    val = np.log(rpf / mx).astype(np.float32) / np.float32(math.log(MD / mx)) * (NB - mx)
    lg = mx + val.astype(np.int32)
    lg = np.minimum(lg, NB - 1)
    return np.where(small, rp, lg)


def _host_onehot_win():
    deltas = np.arange(-(S - 1), S)
    buckets = _bucket_of_delta(deltas)
    M = np.zeros((32, len(deltas)), np.float32)
    M[buckets, np.arange(len(deltas))] = 1.0
    M[0, :] -= 1.0
    return np.ascontiguousarray(M[:, 1920:2432]).astype(BFD)


class _PEWrap:
    def __init__(self, e):
        self._e = e

    def matmul(self, *a, **k):
        k.setdefault("skip_group_check", True)
        return self._e.matmul(*a, **k)

    def __getattr__(self, n):
        return getattr(self._e, n)


class Asm:
    def __init__(self, nc, sems):
        self.nc = nc
        self.sems = sems
        self.cnt = {s: 0 for s in sems}
        self.mile = {}
        self.ops = []
        self.uid = 0

    def new_block(self):
        self.ops = []

    def op(self, eng, fn, sem=None, n=1, waits=()):
        if sem is not None:
            self.cnt[sem] += n
        self.ops.append((eng, fn, sem, n, tuple(w for w in waits if w is not None)))

    def mark(self, sem):
        self.uid += 1
        name = f"m{self.uid}"
        self.mile[name] = (sem, self.cnt[sem])
        return name

    def emit(self, block):
        mile, sems, ops = self.mile, self.sems, self.ops

        def body(engine, eng_name):
            if eng_name == "pe":
                engine = _PEWrap(engine)
            waited = {}
            for (eng, fn, sem, n, waits) in ops:
                if eng != eng_name:
                    continue
                for w in waits:
                    s, v = mile[w] if isinstance(w, str) else w
                    if waited.get(s, -1) >= v:
                        continue
                    waited[s] = v
                    engine.wait_ge(sems[s], v)
                ins = fn(engine)
                if sem is not None:
                    ins.then_inc(sems[sem], n)

        for nm in ("gp", "sy", "pe", "act", "dve"):
            if any(o[0] == nm for o in ops):
                getattr(block, {"gp": "gpsimd", "sy": "sync", "pe": "tensor",
                                "act": "scalar", "dve": "vector"}[nm])(
                    lambda e, nm=nm: body(e, nm))


def build_program(detect_races=True):
    nc = bass.Bass("TRN2", target_bir_lowering=False,
                   detect_race_conditions=detect_races)
    es = ExitStack()

    dt_x = nc.dram_tensor("x", [NC, 128, S], F32, kind="ExternalInput").ap()
    dt_wg = nc.dram_tensor("wg", [NPR, 128, 16, 256], U8, kind="ExternalInput").ap()
    dt_wu = nc.dram_tensor("wu", [NPR, 128, 16, 256], U8, kind="ExternalInput").ap()
    dt_wd = nc.dram_tensor("wd", [NPR, 128, 2, D], U8, kind="ExternalInput").ap()
    dt_wq = nc.dram_tensor("wqa", [8, 128, 16, 256], F32, kind="ExternalInput").ap()
    dt_wk = nc.dram_tensor("wka", [8, 128, 16, 256], F32, kind="ExternalInput").ap()
    dt_wv = nc.dram_tensor("wva", [8, 128, 16, 256], F32, kind="ExternalInput").ap()
    dt_wo = nc.dram_tensor("woa", [16, 128, D], F32, kind="ExternalInput").ap()
    dt_rb = nc.dram_tensor("relb", [32, 32], F32, kind="ExternalInput").ap()
    dt_sg = nc.dram_tensor("sgsd", [128, NHC], F32, kind="ExternalInput").ap()
    dt_su = nc.dram_tensor("susd", [128, NHC], F32, kind="ExternalInput").ap()
    dt_sd = nc.dram_tensor("sdsd", [128, NC], F32, kind="ExternalInput").ap()
    dt_idb = nc.dram_tensor("idbd", [128, 128], BF16, kind="ExternalInput").ap()
    dt_ones = nc.dram_tensor("ones128d", [128, 1], BF16, kind="ExternalInput").ap()
    dt_onesr = nc.dram_tensor("onesrowd", [1, 128], BF16, kind="ExternalInput").ap()
    dt_neg = nc.dram_tensor("neg128rd", [1, 128], BF16, kind="ExternalInput").ap()
    dt_oh = nc.dram_tensor("onehotd", [32, 512], BF16, kind="ExternalInput").ap()
    dt_out = nc.dram_tensor("out", [NC, 128, NTB, 512], F32, kind="ExternalOutput").ap()
    d_toep = nc.dram_tensor("toepd", [32, 512], BF16, kind="Internal").ap()
    dt_tap1 = dt_tap2 = None
    DBG = {}
    if DEBUG_TAPS:
        dt_tap1 = nc.dram_tensor("tap1", [NC, 128, S], F32, kind="ExternalOutput").ap()
        dt_tap2 = nc.dram_tensor("tap2", [NC, 128, S], F32, kind="ExternalOutput").ap()
        DBG["xn"] = nc.dram_tensor("dbg_xn", [128, 8192], F32, kind="ExternalOutput").ap()
        DBG["hb"] = nc.dram_tensor("dbg_hb", [128, NHC * 512], F32, kind="ExternalOutput").ap()
        DBG["mr"] = nc.dram_tensor("dbg_mr", [1, S], F32, kind="ExternalOutput").ap()
        DBG["rs"] = nc.dram_tensor("dbg_rs", [1, S], F32, kind="ExternalOutput").ap()
        DBG["cs"] = nc.dram_tensor("dbg_cs", [1, 512], F32, kind="ExternalOutput").ap()
        DBG["var"] = nc.dram_tensor("dbg_var", [8, 512], F32, kind="ExternalOutput").ap()
        DBG["q"] = nc.dram_tensor("dbg_q", [128, 2, S], F32, kind="ExternalOutput").ap()
        DBG["k"] = nc.dram_tensor("dbg_k", [128, 2, S], F32, kind="ExternalOutput").ap()
        DBG["va"] = nc.dram_tensor("dbg_va", [128, 16, 260], F32, kind="ExternalOutput").ap()
        DBG["ab"] = nc.dram_tensor("dbg_ab", [128, NC, S], F32, kind="ExternalOutput").ap()
        DBG["bm"] = nc.dram_tensor("dbg_bm", [128, 4], F32, kind="ExternalOutput").ap()
        DBG["e0"] = nc.dram_tensor("dbg_e0", [128, 1024], F32, kind="ExternalOutput").ap()
        DBG["pv"] = nc.dram_tensor("dbg_pv", [128, 1024], F32, kind="ExternalOutput").ap()
        DBG["mx"] = nc.dram_tensor("dbg_mx", [128, 16], F32, kind="ExternalOutput").ap()

    xbuf = nc.alloc_sbuf_tensor("xbuf", [128, NC, S], BF16).ap()
    rstd_b = nc.alloc_sbuf_tensor("rstd_b", [128, S], BF16).ap()
    m_row = nc.alloc_sbuf_tensor("m_row", [128, S], BF16).ap()
    row_bf = nc.alloc_sbuf_tensor("row_bf", [128, 512], BF16).ap()
    scrA = nc.alloc_sbuf_tensor("scrA", [1, 512], F32).ap()
    scrB = nc.alloc_sbuf_tensor("scrB", [1, 512], F32).ap()
    r31row = nc.alloc_sbuf_tensor("r31row", [128, 32], BF16).ap()
    idb = nc.alloc_sbuf_tensor("idb", [128, 128], BF16).ap()
    ones128 = nc.alloc_sbuf_tensor("ones128", [128, 1], BF16).ap()
    onesr = nc.alloc_sbuf_tensor("onesr", [1, 128], BF16).ap()
    negr = nc.alloc_sbuf_tensor("negr", [1, 128], BF16).ap()
    negrz = nc.alloc_sbuf_tensor("negrz", [128, 128], BF16).ap()
    onesrz = nc.alloc_sbuf_tensor("onesrz", [128, 128], BF16).ap()
    sgs = nc.alloc_sbuf_tensor("sgs", [128, NHC], F32).ap()
    sus = nc.alloc_sbuf_tensor("sus", [128, NHC], F32).ap()
    sdh = nc.alloc_sbuf_tensor("sdh", [128, NC], F32).ap()
    rstd_col = nc.alloc_sbuf_tensor("rstd_col", [128, NC], F32).ap()
    mx1 = nc.alloc_sbuf_tensor("mx1", [128, 1], BF16).ap()
    bm = [nc.alloc_sbuf_tensor(f"bm{i}", [128, 1], F32).ap() for i in range(4)]
    bm31 = [nc.alloc_sbuf_tensor(f"bm31{i}", [128, 1], F32).ap() for i in range(4)]
    rrow_bf = nc.alloc_sbuf_tensor("rrow_bf", [128, 1024], BF16).ap()
    srow = nc.alloc_sbuf_tensor("srow", [128, 8], BF16).ap()
    c_eps = nc.alloc_sbuf_tensor("c_eps", [1, 1], F32).ap()
    c_mhalf = nc.alloc_sbuf_tensor("c_mhalf", [1, 1], F32).ap()

    pAll = nc.alloc_psum_tensor("pAll", [128, 3584], F32).ap()
    pTrb = nc.alloc_psum_tensor("pTrb", [128, 1024], BF16).ap()

    def bank(i, w=512):
        return pAll[:, i * 512: i * 512 + w]

    sems = {}
    for s in ("sp", "sa", "sv", "sc", "sx", "srb", "stw", "stp",
              "sw0", "sw1", "sw2", "sw3", "swq0", "swq1",
              "so0", "so1", "so2", "so3", "so4", "so5", "so6", "so7", "sdb", "sdt"):
        sems[s] = es.enter_context(nc.semaphore(s))
    A = Asm(nc, sems)
    M = A.mark

    slot_last = {}
    bank_last = {}
    mbst = {}

    def bank_waits(*bs):
        return [bank_last.get(b) for b in bs]

    def stats_pass(tag, sq_bufs, src_ready=None):
        sq_marks = {}
        def srcw(dc):
            if isinstance(src_ready, dict):
                return src_ready.get(dc)
            return src_ready
        for half in range(2):
            tbs = (2 * half, 2 * half + 1)
            grp_marks = []
            for dc in range(NC):
                sq = sq_bufs[dc % 2]
                A.op("act", lambda e, sq=sq, dc=dc, half=half: e.activation(
                    sq, xbuf[:, dc, half * 1024:(half + 1) * 1024], AF.Square),
                    sem="sa", waits=[srcw(dc), sq_marks.get(dc % 2)])
                msq = M("sa")
                for j, tb in enumerate(tbs):
                    bsum, bsq = 2 * j, 2 * j + 1
                    A.op("pe", lambda e, dc=dc, tb=tb, b=bsum: e.matmul(
                        pAll[0:1, b * 512:b * 512 + 512], ones128[:, 0:1],
                        xbuf[:, dc, tb * 512:(tb + 1) * 512],
                        start=(dc == 0), stop=(dc == NC - 1)),
                        sem="sp" if dc == NC - 1 else None,
                        waits=[srcw(dc)] + (bank_waits(bsum) if dc == 0 else []))
                    if dc == NC - 1:
                        grp_marks.append((bsum, M("sp")))
                    last_sq = (j == len(tbs) - 1)
                    A.op("pe", lambda e, sq=sq, tb=tb, b=bsq, dc=dc, half=half:
                         e.matmul(
                        pAll[0:1, b * 512:b * 512 + 512], ones128[:, 0:1],
                        sq[:, (tb - 2 * half) * 512:(tb - 2 * half + 1) * 512],
                        start=(dc == 0), stop=(dc == NC - 1)),
                        sem="sp" if (dc == NC - 1 or last_sq) else None,
                        waits=([msq] + bank_waits(bsq)) if dc == 0 else [msq])
                    if dc == NC - 1 and last_sq:
                        mk = M("sp"); grp_marks.append((bsq, mk)); sq_marks[dc % 2] = mk
                    elif dc == NC - 1:
                        grp_marks.append((bsq, M("sp")))
                    elif last_sq:
                        sq_marks[dc % 2] = M("sp")
            pe_done = grp_marks[-1][1]
            for j, tb in enumerate(tbs):
                bsum, bsq = 2 * j, 2 * j + 1
                sl = slice(tb * 512, (tb + 1) * 512)
                A.op("dve", lambda e, b=bsum: e.tensor_scalar(
                    scrA, pAll[0:1, b * 512:b * 512 + 512], 1.0 / 2048.0, None,
                    ALU.mult), sem="sv", waits=[pe_done])
                A.op("dve", lambda e, sl=sl: e.tensor_copy(m_row[0:1, sl], scrA),
                     sem="sv")
                mbst[f"mrow_{tag}_{tb}"] = M("sv")
                A.op("dve", lambda e: e.tensor_mul(scrB, scrA, scrA), sem="sv")
                A.op("dve", lambda e, b=bsq: e.tensor_scalar(
                    scrA, pAll[0:1, b * 512:b * 512 + 512], 1.0 / 2048.0, None,
                    ALU.mult), sem="sv")
                if DEBUG_TAPS and tag == "ln1":
                    A.op("dve", lambda e: e.tensor_copy(DBG["scr"], scrA),
                         sem="sv", waits=[mbst.get("dbgscr")])
                    dsc = M("sv")
                    A.op("gp", lambda e, tb=tb: e.dma_start(
                        out=DBG["var"][tb:tb + 1, :], in_=DBG["scr"]),
                        sem="sdb", n=16, waits=[dsc])
                    mbst["dbgscr"] = M("sdb")
                A.op("dve", lambda e: e.tensor_sub(scrA, scrA, scrB), sem="sv")
                vr = M("sv")
                if DEBUG_TAPS and tag == "ln1":
                    A.op("dve", lambda e: e.tensor_copy(DBG["scr"], scrA),
                         sem="sv", waits=[mbst.get("dbgscr")])
                    dsc = M("sv")
                    A.op("gp", lambda e, tb=tb: e.dma_start(
                        out=DBG["var"][4 + tb:5 + tb, :], in_=DBG["scr"]),
                        sem="sdb", n=16, waits=[dsc])
                    mbst["dbgscr"] = M("sdb")
                bank_last[bsum] = vr
                bank_last[bsq] = vr
                A.op("act", lambda e: e.activation(scrB, scrA, AF.Ln,
                                                   bias=c_eps[0:1, 0:1]),
                     sem="sa", waits=[vr, mbst.get("cconst")])
                A.op("act", lambda e: e.activation(scrA, scrB, AF.Exp,
                                                   scale=c_mhalf[0:1, 0:1]),
                     sem="sa")
                rstd_m = M("sa")
                A.op("dve", lambda e: e.tensor_copy(row_bf[0:1, :], scrA),
                     sem="sv", waits=[rstd_m, mbst.get("rowbf_guard")])
                rb_m = M("sv")
                A.op("pe", lambda e: e.matmul(bank(4), onesrz, row_bf,
                                              start=True, stop=True), sem="sp",
                     waits=[rb_m] + bank_waits(4))
                bc = M("sp")
                mbst["rowbf_guard"] = bc
                A.op("act", lambda e, sl=sl: e.activation(rstd_b[:, sl], bank(4),
                                                          AF.Copy), sem="sa",
                     waits=[bc])
                bank_last[4] = M("sa")
                mbst[f"rstd_{tag}_{tb}"] = bank_last[4]
            for b, mk in grp_marks:
                bank_last[b] = mk
        A.op("dve", lambda e: e.engine_nop(), sem="sv")
        last = M("sv")
        for b in range(4):
            bank_last[b] = last
        mbst[f"stats_{tag}"] = last

    def bcast_mb(tag, m_b):
        for tb in range(NTB):
            sl = slice(tb * 512, (tb + 1) * 512)
            A.op("pe", lambda e, sl=sl: e.matmul(
                bank(4), onesrz, m_row[:, sl], start=True, stop=True),
                sem="sp", waits=[mbst[f"mrow_{tag}_{tb}"]] + bank_waits(4))
            bc = M("sp")
            A.op("act", lambda e, sl=sl: e.activation(m_b[:, sl], bank(4), AF.Copy),
                 sem="sa", waits=[bc])
            bank_last[4] = M("sa")
        mbst[f"mb_{tag}"] = bank_last[4]

    def ffn_pass(tag, bufs, final):
        m_b, hbuf, wsl, xn_tt, utmp, stage, cs_row, hs_hi, hs_lo, xn32 = bufs
        for tt in range(NTB):
            sl = slice(tt * 512, (tt + 1) * 512)
            stg_w = [mbst[f"rstd_{tag}_{tt}"], mbst[f"mb_{tag}"],
                     slot_last.get("xn")]
            stg_m = {}
            for dc in range(NC):
                xs = xn_tt[:, dc * 512:(dc + 1) * 512]
                A.op("dve", lambda e, dc=dc, sl=sl: e.tensor_sub(
                    xn32, xbuf[:, dc, sl], m_b[:, sl]), sem="sv",
                    waits=stg_w if dc == 0 else [])
                A.op("dve", lambda e, xs=xs, sl=sl: e.tensor_mul(
                    xs, xn32, rstd_b[:, sl]), sem="sv")
                stg_m[dc] = M("sv")
            xn_ready = stg_m[NC - 1]
            if DEBUG_TAPS and tag == "ln1" and tt == 0 and not final:
                A.op("gp", lambda e: e.dma_start(out=DBG["xn"], in_=xn_tt),
                     sem="sdt", n=16, waits=[xn_ready])
            for dc in range(NC):
                A.op("pe", lambda e, dc=dc: e.matmul(
                    pAll[0:1, 6 * 512:7 * 512], ones128[:, 0:1],
                    xn_tt[:, dc * 512:(dc + 1) * 512],
                    start=(dc == 0), stop=(dc == NC - 1)),
                    sem="sp" if dc == NC - 1 else None,
                    waits=[stg_m[dc]] + (bank_waits(6) if dc == 0 else []))
            csm = M("sp")
            A.op("dve", lambda e: e.tensor_copy(cs_row[0:1, :],
                                                pAll[0:1, 6 * 512:7 * 512]),
                 sem="sv", waits=[csm])
            bank_last[6] = M("sv")
            cs_ready = bank_last[6]
            hm = None
            for pr in range(NPR):
                gslot, gname = wsl[pr % 2], f"wsl{pr % 2}"
                uslot, uname = wsl[2 + pr % 2], f"wsl{2 + pr % 2}"
                gsem = f"sw{pr % 2}"
                usem = f"sw{2 + pr % 2}"
                A.op("gp", lambda e, pr=pr, gslot=gslot: e.dma_start(
                    out=gslot.rearrange("p (c h) -> p c h", c=16),
                    in_=dt_wg[pr]),
                    sem=gsem, n=16, waits=[slot_last.get(gname)])
                gdma = M(gsem)
                A.op("gp", lambda e, pr=pr, uslot=uslot: e.dma_start(
                    out=uslot.rearrange("p (c h) -> p c h", c=16),
                    in_=dt_wu[pr]),
                    sem=usem, n=16, waits=[slot_last.get(uname)])
                udma = M(usem)
                for k in range(2):
                    hc = 2 * pr + k
                    gb, ub = hc % 2, 2 + hc % 2
                    hsl = slice(hc * 512, (hc + 1) * 512)
                    for c in range(NC):
                        A.op("pe", lambda e, c=c, k=k, gb=gb, gslot=gslot: e.matmul(
                            bank(gb),
                            gslot[:, c * 256 + k * 128: c * 256 + k * 128 + 128],
                            xn_tt[:, c * 512:(c + 1) * 512],
                            start=(c == 0), stop=False),
                            waits=([gdma, stg_m[c]] + bank_waits(gb)) if c == 0 else [])
                    A.op("pe", lambda e, gb=gb: e.matmul(
                        bank(gb), negrz, cs_row, start=False,
                        stop=True), sem="sp", waits=[cs_ready])
                    gmm = M("sp")
                    if k == 1:
                        slot_last[gname] = gmm
                    A.op("act", lambda e, hsl=hsl, gb=gb, hc=hc: e.activation(
                        hbuf[:, hsl], bank(gb), AF.Silu, scale=sgs[:, hc:hc + 1]),
                        sem="sa", waits=[gmm])
                    silu_m = M("sa")
                    bank_last[gb] = silu_m
                    for c in range(NC):
                        A.op("pe", lambda e, c=c, k=k, ub=ub, uslot=uslot: e.matmul(
                            bank(ub),
                            uslot[:, c * 256 + k * 128: c * 256 + k * 128 + 128],
                            xn_tt[:, c * 512:(c + 1) * 512],
                            start=(c == 0), stop=False),
                            waits=([udma] + bank_waits(ub)) if c == 0 else [])
                    A.op("pe", lambda e, ub=ub: e.matmul(
                        bank(ub), negrz, cs_row, start=False,
                        stop=True), sem="sp")
                    umm = M("sp")
                    if k == 1:
                        slot_last[uname] = umm
                    if hc == NHC - 1:
                        slot_last["xn"] = umm
                    A.op("act", lambda e, hc=hc, ub=ub: e.activation(
                        utmp[hc % 2], bank(ub), AF.Copy, scale=sus[:, hc:hc + 1]),
                        sem="sa", waits=[umm, slot_last.get(f"utmp{hc % 2}")])
                    up_m = M("sa")
                    bank_last[ub] = up_m
                    A.op("dve", lambda e, hsl=hsl, hc=hc: e.tensor_mul(
                        hbuf[:, hsl], hbuf[:, hsl], utmp[hc % 2]), sem="sv",
                        waits=[silu_m, up_m])
                    hm = M("sv")
                    slot_last[f"utmp{hc % 2}"] = hm
            if DEBUG_TAPS and tag == "ln1" and tt == 0 and not final:
                A.op("gp", lambda e: e.dma_start(out=DBG["hb"], in_=hbuf),
                     sem="sdt", n=16, waits=[hm])
                A.op("gp", lambda e: e.dma_start(out=DBG["mr"], in_=m_row),
                     sem="sdt", n=16)
                A.op("gp", lambda e: e.dma_start(out=DBG["rs"], in_=rstd_b[0:1, :]),
                     sem="sdt", n=16)
                A.op("gp", lambda e: e.dma_start(out=DBG["cs"], in_=cs_row[0:1, :]),
                     sem="sdt", n=16)
            for hc in range(NHC):
                A.op("pe", lambda e, hc=hc: e.matmul(
                    pAll[0:1, 6 * 512:7 * 512], ones128[:, 0:1],
                    hbuf[:, hc * 512:(hc + 1) * 512],
                    start=(hc == 0), stop=(hc == NHC - 1)),
                    sem="sp" if hc == NHC - 1 else None,
                    waits=([hm] + bank_waits(6)) if hc == 0 else [])
            hsm = M("sp")
            A.op("dve", lambda e: e.tensor_copy(hs_hi[0:1, :],
                                                pAll[0:1, 6 * 512:7 * 512]),
                 sem="sv", waits=[hsm])
            A.op("dve", lambda e: e.tensor_sub(hs_lo[0:1, :],
                                               pAll[0:1, 6 * 512:7 * 512],
                                               hs_hi[0:1, :]), sem="sv")
            bank_last[6] = M("sv")
            hs_ready = bank_last[6]
            for grp in ([0, 1, 2, 3, 4, 5, 6], [7, 8, 9, 10, 11, 12, 13], [14, 15]):
                for pr in range(NPR):
                    dslot, dname = wsl[pr % 4], f"wsl{pr % 4}"
                    dsem = f"sw{pr % 4}"
                    A.op("gp", lambda e, pr=pr, dslot=dslot: e.dma_start(
                        out=dslot.rearrange("p (a d) -> p a d", a=2),
                        in_=dt_wd[pr]),
                        sem=dsem, n=16, waits=[slot_last.get(dname)])
                    ddma = M(dsem)
                    for k in range(2):
                        hc = 2 * pr + k
                        for bi, dc in enumerate(grp):
                            w = []
                            if hc == 0:
                                w += [hs_ready] + bank_waits(bi)
                            if k == 0 and bi == 0:
                                w += [ddma]
                            is_last = (k == 1 and bi == len(grp) - 1)
                            A.op("pe", lambda e, k=k, dc=dc, bi=bi, dslot=dslot,
                                 hc=hc: e.matmul(
                                     bank(bi),
                                     dslot[:, k * 2048 + dc * 128:
                                           k * 2048 + dc * 128 + 128],
                                     hbuf[:, hc * 512:(hc + 1) * 512],
                                     start=(hc == 0), stop=False),
                                 sem="sp" if is_last else None, waits=w)
                            if is_last:
                                slot_last[dname] = M("sp")
                for bi, dc in enumerate(grp):
                    A.op("pe", lambda e, bi=bi: e.matmul(
                        bank(bi), negrz, hs_hi, start=False,
                        stop=False))
                    A.op("pe", lambda e, bi=bi: e.matmul(
                        bank(bi), negrz, hs_lo, start=False,
                        stop=True), sem="sp")
                    dmm = M("sp")
                    if not final:
                        A.op("dve", lambda e, dc=dc, bi=bi, sl=sl:
                             e.scalar_tensor_tensor(
                                 xbuf[:, dc, sl], bank(bi), sdh[:, dc:dc + 1],
                                 xbuf[:, dc, sl], ALU.mult, ALU.add),
                             sem="sv", waits=[dmm])
                        bank_last[bi] = M("sv")
                        if tt == NTB - 1:
                            mbst[f"evlast_{dc}"] = bank_last[bi]
                    else:
                        st = stage[dc % 6]
                        A.op("dve", lambda e, bi=bi, sl=sl, st=st, dc=dc:
                             e.scalar_tensor_tensor(
                                 st, bank(bi), sdh[:, dc:dc + 1],
                                 xbuf[:, dc, sl], ALU.mult, ALU.add),
                             sem="sv", waits=[dmm, slot_last.get(f"stage{dc % 6}")])
                        ev = M("sv")
                        bank_last[bi] = ev
                        ssem = f"so{dc % 6}"
                        A.op("sy", lambda e, dc=dc, tt=tt, st=st: e.dma_start(
                            out=dt_out[dc, :, tt, :], in_=st), sem=ssem, n=16,
                            waits=[ev])
                        slot_last[f"stage{dc % 6}"] = M(ssem)
        A.op("dve", lambda e: e.engine_nop(), sem="sv")
        mbst[f"ffn_{tag}_done"] = M("sv")

    ffn_es = ExitStack()
    relb = ffn_es.enter_context(nc.sbuf_tensor("relb_sb", [32, 32], BF16)).ap()
    onehot = ffn_es.enter_context(nc.sbuf_tensor("onehot", [32, 512], BF16)).ap()
    twin = ffn_es.enter_context(nc.sbuf_tensor("twin", [32, 512], BF16)).ap()
    m_b = ffn_es.enter_context(nc.sbuf_tensor("m_b", [128, S], BF16)).ap()
    hbuf = ffn_es.enter_context(nc.sbuf_tensor("hbuf", [128, NHC * 512], BF16)).ap()
    wsl = [ffn_es.enter_context(nc.sbuf_tensor(f"wsl{i}", [128, 4096], BF16)).ap()
           for i in range(4)]
    xn_tt = ffn_es.enter_context(nc.sbuf_tensor("xn_tt", [128, 8192], BF16)).ap()
    utmp = [ffn_es.enter_context(nc.sbuf_tensor(f"utmp{i}", [128, 512], BF16)).ap()
            for i in range(2)]
    sq0 = ffn_es.enter_context(nc.sbuf_tensor("sq0", [128, S // 2], BF16)).ap()
    sq1 = ffn_es.enter_context(nc.sbuf_tensor("sq1", [128, S // 2], BF16)).ap()
    stage = [ffn_es.enter_context(nc.sbuf_tensor(f"stage{i}", [128, 512], F32)).ap()
             for i in range(6)]
    cs_row = ffn_es.enter_context(nc.sbuf_tensor("cs_row", [128, 512], BF16)).ap()
    if DEBUG_TAPS:
        DBG["scr"] = ffn_es.enter_context(nc.sbuf_tensor("dbg_scr", [1, 512], F32)).ap()
    hs_hi = ffn_es.enter_context(nc.sbuf_tensor("hs_hi", [128, 512], BF16)).ap()
    hs_lo = ffn_es.enter_context(nc.sbuf_tensor("hs_lo", [128, 512], BF16)).ap()
    xn32 = ffn_es.enter_context(nc.sbuf_tensor("xn32", [128, 512], F32)).ap()
    ffn_bufs = (m_b, hbuf, wsl, xn_tt, utmp, stage, cs_row, hs_hi, hs_lo, xn32)

    A.new_block()
    for sb, dr in ((idb, dt_idb), (ones128, dt_ones), (onesr, dt_onesr),
                   (negr, dt_neg), (sgs, dt_sg), (sus, dt_su), (sdh, dt_sd),
                   (onehot, dt_oh)):
        A.op("sy", lambda e, sb=sb, dr=dr: e.dma_start(out=sb, in_=dr),
             sem="sc", n=16)
    consts_done = M("sc")
    A.op("dve", lambda e: e.tensor_scalar(sdh, sdh, 0.5, None, ALU.mult),
         sem="sv", waits=[consts_done])
    A.op("dve", lambda e: e.memset(c_eps, 1e-5), sem="sv")
    A.op("dve", lambda e: e.memset(c_mhalf, -0.5), sem="sv")
    A.op("dve", lambda e: e.memset(negrz, 0.0), sem="sv")
    A.op("dve", lambda e: e.memset(onesrz, 0.0), sem="sv")
    for zb in (m_row, row_bf, r31row, rrow_bf, srow):
        A.op("dve", lambda e, zb=zb: e.memset(zb, 0.0), sem="sv")
    A.op("dve", lambda e: e.tensor_copy(negrz[0:1, :], negr), sem="sv",
         waits=[consts_done])
    A.op("dve", lambda e: e.tensor_copy(onesrz[0:1, :], onesr), sem="sv")
    mbst["cconst"] = M("sv")
    for dc in range(NC):
        A.op("gp", lambda e, dc=dc: e.dma_start(out=xbuf[:, dc, :], in_=dt_x[dc]),
             sem="sx", n=16)
    x_loaded = M("sx")
    A.op("gp", lambda e: e.dma_start(out=relb, in_=dt_rb), sem="srb", n=16)
    rb_done = M("srb")

    A.op("pe", lambda e: e.matmul(pAll[0:32, 5 * 512:6 * 512], relb, onehot,
                                  start=True, stop=True),
         sem="sp", waits=[consts_done, rb_done])
    tprime = M("sp")
    A.op("act", lambda e: e.activation(twin, pAll[0:32, 5 * 512:6 * 512], AF.Copy),
         sem="sa", waits=[tprime])
    twin_done = M("sa")
    bank_last[5] = twin_done
    A.op("sy", lambda e: e.dma_start(out=d_toep[:, 0:384], in_=twin[0:32, 0:384]),
         sem="stw", n=16, waits=[twin_done])
    mbst["toepw"] = M("stw")
    A.op("pe", lambda e: e.transpose(pTrb[0:1, 0:32], twin[0:32, 240:241],
                                     idb[0:32, 0:32]), sem="sp",
         waits=[twin_done, consts_done])
    trp = M("sp")
    A.op("dve", lambda e: e.tensor_copy(r31row[0:1, :], pTrb[0:1, 0:32]),
         sem="sv", waits=[trp, mbst["cconst"]])
    bank_last[7] = M("sv")
    mbst["r31"] = bank_last[7]

    stats_pass("ln1", (sq0, sq1), src_ready=x_loaded)
    bcast_mb("ln1", m_b)

    with nc.Block("b0") as blk:
        A.emit(blk)

    A.new_block()
    for buf_ in (cs_row, hs_hi, hs_lo):
        A.op("dve", lambda e, buf_=buf_: e.memset(buf_, 0.0), sem="sv")
    ffn_pass("ln1", ffn_bufs, final=False)
    stats_pass("ln2", (sq0, sq1),
               src_ready={dc: mbst[f"evlast_{dc}"] for dc in range(NC)})
    with nc.Block("b1") as blk:
        A.emit(blk)

    ffn_es.close()
    at_es = ExitStack()
    toep = at_es.enter_context(nc.sbuf_tensor("toep", [128, 2, 4, 256], BF16)).ap()
    wqkv = [at_es.enter_context(nc.sbuf_tensor(f"wqkv{i}", [128, 4096], BF16)).ap()
            for i in range(2)]
    Qsg = at_es.enter_context(nc.sbuf_tensor("Qsg", [128, 2, S], BF16)).ap()
    Ksg = at_es.enter_context(nc.sbuf_tensor("Ksg", [128, 2, S], BF16)).ap()
    Vaug = at_es.enter_context(nc.sbuf_tensor("Vaug", [128, 16, 260], BF16)).ap()
    Ebuf = [at_es.enter_context(nc.sbuf_tensor(f"Ebuf{i}", [128, 1024], BF16)).ap()
            for i in range(2)]
    rb_sb = at_es.enter_context(nc.sbuf_tensor("rb_sb", [128, 1024], BF16)).ap()
    Ksub = at_es.enter_context(nc.sbuf_tensor("Ksub", [128, 2, 128], BF16)).ap()
    Qsub = at_es.enter_context(nc.sbuf_tensor("Qsub", [128, 128], BF16)).ap()
    ncsw = at_es.enter_context(nc.sbuf_tensor("ncsw", [128, 256], BF16)).ap()
    abuf = at_es.enter_context(nc.sbuf_tensor("abuf", [128, NC, S], BF16)).ap()
    Kpad = at_es.enter_context(nc.sbuf_tensor("Kpad", [128, 2, S], BF16)).ap()

    A.new_block()
    A.op("dve", lambda e: e.memset(Kpad, 0.0), sem="sv")
    A.op("dve", lambda e: e.memset(ncsw, 0.0), sem="sv")
    A.op("dve", lambda e: e.memset(Ksub, 0.0), sem="sv")
    A.op("dve", lambda e: e.memset(Qsub, 0.0), sem="sv")
    if DEBUG_TAPS:
        for dc in range(NC):
            A.op("gp", lambda e, dc=dc: e.dma_start(out=dt_tap1[dc],
                                                    in_=xbuf[:, dc, :]),
                 sem="sdt", n=16)
        tap1_done = M("sdt")
    def emit_toep(s):
        sl = s % 2
        for p in range(128):
            w = ([mbst.get(f"toepu{sl}"),
                  mbst["toepw"] if s < 2 else None] if p < 2 else [])
            A.op("sy", lambda e, p=p, s=s, sl=sl: e.dma_start(
                out=toep[p:p + 1, sl, :, :],
                in_=d_toep[s * 4:(s + 1) * 4,
                           127 - p:127 - p + 256].unsqueeze(0)),
                sem="stp", n=16, waits=w)
        mbst[f"toep_{s}"] = M("stp")

    emit_toep(0)
    emit_toep(1)

    for c in range(NC):
        A.op("pe", lambda e, c=c: e.transpose(
            pTrb[:, 0:128], rstd_b[:, c * 128:(c + 1) * 128], idb), sem="sp",
            waits=bank_waits(7))
        tm = M("sp")
        A.op("dve", lambda e, c=c: e.tensor_copy(rstd_col[:, c:c + 1],
                                                 pTrb[:, 0:1]), sem="sv",
             waits=[tm])
        bank_last[7] = M("sv")
    rstdcol_done = bank_last[7]

    exp_last = {}
    aev = None

    def emit_kpad(h_t):
        r0t = (h_t % 2) * 64
        A.op("dve", lambda e, h_t=h_t, r0t=r0t: e.tensor_copy(
            Kpad[r0t:r0t + 64, h_t % 2, :],
            Ksg[r0t:r0t + 64, h_t // 2, :]), sem="sv",
            waits=[mbst.get(f"smm_par{h_t % 2}")])
        mbst[f"kpad{h_t % 2}"] = M("sv")

    for sg in range(8):
        def emit_wdma(i, dt_w):
            j = 3 * sg + i
            buf = wqkv[j % 2]
            qsem = f"swq{j % 2}"
            A.op("gp", lambda e, dt_w=dt_w, sg=sg, buf=buf: e.dma_start(
                out=buf.rearrange("p (c h) -> p c h", c=16),
                in_=dt_w[sg]),
                sem=qsem, n=16,
                waits=[slot_last.get(f"wqkv{j % 2}"),
                       mbst.get("pass1_done") if (i == 0 and EXP_DELAY) else None])
            return M(qsem)

        wdma = [emit_wdma(0, dt_wq), emit_wdma(1, dt_wk)]
        projm = {}
        def make_p1_stages():
            stages = []
            state = {}

            def s1(h):
                oc = h // 2
                r0 = (h % 2) * 64
                A.op("dve", lambda e, oc=oc, r0=r0, hp=h % 2: e.tensor_copy(
                    Ksub[r0:r0 + 64, hp, :], Ksg[r0:r0 + 64, oc, 0:S:SUBS]),
                    sem="sv", waits=[projm["k"], mbst.get("ksub_guard"),
                                     mbst.get("dbgqkv") if DEBUG_TAPS else None])
                A.op("dve", lambda e, oc=oc, r0=r0: e.tensor_copy(
                    Qsub[r0:r0 + 64, :], Qsg[r0:r0 + 64, oc, 0:S:SUBS]),
                    sem="sv", waits=[projm["q"]])
                state["sub"] = M("sv")

            def s2(h):
                r0 = (h % 2) * 64
                A.op("pe", lambda e, hp=h % 2: e.matmul(
                    pAll[:, 0:128], Qsub, Ksub[:, hp, :],
                    start=True, stop=True), sem="sp",
                    waits=[state["sub"]] + bank_waits(0))
                p1 = M("sp")
                mbst["ksub_guard"] = p1
                A.op("dve", lambda e: e.tensor_reduce(
                    mx1, pAll[:, 0:128], AX.X, ALU.max), sem="sv", waits=[p1])
                state["mx"] = M("sv")
                bank_last[0] = state["mx"]

            def s3(h):
                A.op("pe", lambda e: e.transpose(pTrb[0:1, 0:128], mx1, idb),
                     sem="sp", waits=[state["mx"]] + bank_waits(7))
                trm = M("sp")
                A.op("dve", lambda e: e.tensor_reduce(
                    srow[0:1, 0:1], pTrb[0:1, 0:128], AX.X, ALU.max),
                    sem="sv", waits=[trm])
                bank_last[7] = M("sv")
                state["srow"] = bank_last[7]

            def s4(h):
                par = h
                gh = sg * 4 + h
                A.op("pe", lambda e: e.matmul(pAll[:, 0:1], onesrz,
                                              srow[:, 0:1], start=True,
                                              stop=True), sem="sp",
                     waits=[state["srow"]] + bank_waits(0))
                bmm = M("sp")
                A.op("dve", lambda e, par=par: e.tensor_scalar(
                    bm[par], pAll[:, 0:1], -1.0, -MARGIN, ALU.mult, ALU.add),
                    sem="sv", waits=[bmm, exp_last.get(par)])
                bm_m = M("sv")
                bank_last[0] = bm_m
                A.op("pe", lambda e, gh=gh: e.matmul(
                    pAll[:, 0:1], onesrz, r31row[:, gh:gh + 1],
                    start=True, stop=True), sem="sp",
                    waits=[mbst["r31"], bm_m] + bank_waits(0))
                r31m = M("sp")
                A.op("dve", lambda e, par=par: e.tensor_add(
                    bm31[par], pAll[:, 0:1], bm[par]), sem="sv", waits=[r31m])
                bm31_m = M("sv")
                bank_last[0] = bm31_m
                mbst[f"bm31m_{h}"] = bm31_m
                mbst["pass1_done"] = bm31_m

            for h in range(4):
                for s in (s1, s2, s3, s4):
                    stages.append(lambda s=s, h=h: s(h))
            return stages

        mbst["p1_stages"] = make_p1_stages()
        for i, name in enumerate(("q", "k", "v")):
            buf = wqkv[(3 * sg + i) % 2]
            for c in range(NC):
                A.op("pe", lambda e, c=c, buf=buf: e.matmul(
                    pAll[0:1, 0:256], ones128[:, 0:1],
                    buf[:, c * 256:(c + 1) * 256],
                    start=(c == 0), stop=(c == NC - 1)),
                    sem="sp" if c == NC - 1 else None,
                    waits=([wdma[i]] + bank_waits(0)) if c == 0 else [])
            cwm = M("sp")
            A.op("dve", lambda e: e.tensor_scalar(ncsw[0:1, :],
                                                  pAll[0:1, 0:256], -1.0,
                                                  None, ALU.mult), sem="sv",
                 waits=[cwm])
            bank_last[0] = M("sv")
            ncsw_ready = bank_last[0]
            if name in ("q", "k"):
                dst = Qsg if name == "q" else Ksg
                pm_last = None
                for oc in range(2):
                    for tb in range(NTB):
                        b = 1 + (oc * NTB + tb) % 2
                        for c in range(NC):
                            A.op("pe", lambda e, c=c, oc=oc, tb=tb, b=b, buf=buf:
                                 e.matmul(
                                     bank(b),
                                     buf[:, c * 256 + oc * 128:
                                         c * 256 + oc * 128 + 128],
                                     xbuf[:, c, tb * 512:(tb + 1) * 512],
                                     start=(c == 0), stop=False),
                                 waits=bank_waits(b) if c == 0 else [])
                        A.op("pe", lambda e, oc=oc, tb=tb, b=b: e.matmul(
                            bank(b), ncsw[:, oc * 128:oc * 128 + 128],
                            m_row[:, tb * 512:(tb + 1) * 512],
                            start=False, stop=True), sem="sp",
                            waits=[ncsw_ready, mbst[f"mrow_ln2_{tb}"]])
                        pm = M("sp")
                        A.op("dve", lambda e, dst=dst, oc=oc, tb=tb, b=b:
                             e.tensor_mul(
                                 dst[:, oc, tb * 512:(tb + 1) * 512], bank(b),
                                 rstd_b[:, tb * 512:(tb + 1) * 512]),
                             sem="sv", waits=[pm, mbst[f"rstd_ln2_{tb}"]])
                        bank_last[b] = M("sv")
                        pm_last = bank_last[b]
                projm[name] = pm_last
                if name == "q":
                    slot_last[f"wqkv{(3 * sg) % 2}"] = projm["q"]
                    wdma.append(emit_wdma(2, dt_wv))
                else:
                    slot_last[f"wqkv{(3 * sg + 1) % 2}"] = projm["k"]
                    emit_kpad(0)
                    emit_kpad(1)
            else:
                A.op("dve", lambda e: e.memset(Vaug, 1.0), sem="sv",
                     waits=[slot_last.get("Vaug"), mbst.get("last_pv")])
                vset = M("sv")
                vm_last = None
                p1_stages = mbst.pop("p1_stages", [])
                for ts in range(16):
                    vb = 3 + ts % 2
                    for c in range(NC):
                        A.op("pe", lambda e, c=c, ts=ts, buf=buf, vb=vb: e.matmul(
                            pAll[:, vb * 512: vb * 512 + 256],
                            xbuf[:, c, ts * 128:(ts + 1) * 128],
                            buf[:, c * 256:(c + 1) * 256],
                            start=(c == 0), stop=False),
                            waits=([vset] + bank_waits(vb)) if c == 0
                            else (bank_waits(vb) if c == 0 else [])) if c == 0 else                         A.op("pe", lambda e, c=c, ts=ts, buf=buf, vb=vb: e.matmul(
                            pAll[:, vb * 512: vb * 512 + 256],
                            xbuf[:, c, ts * 128:(ts + 1) * 128],
                            buf[:, c * 256:(c + 1) * 256],
                            start=(c == 0), stop=False))
                    A.op("pe", lambda e, ts=ts, vb=vb: e.matmul(
                        pAll[:, vb * 512: vb * 512 + 256],
                        m_row[:, ts * 128:(ts + 1) * 128],
                        ncsw[:, 0:256],
                        start=False, stop=True), sem="sp", waits=[ncsw_ready])
                    vm = M("sp")
                    for hh in range(4):
                        A.op("act", lambda e, ts=ts, hh=hh, vb=vb: e.activation(
                            Vaug[:, ts, hh * 65: hh * 65 + 64],
                            pAll[:, vb * 512 + hh * 64: vb * 512 + hh * 64 + 64],
                            AF.Copy, scale=rstd_col[:, ts:ts + 1]), sem="sa",
                            waits=[vm, rstdcol_done] if hh == 0 else [])
                    bank_last[vb] = M("sa")
                    vm_last = bank_last[vb]
                    if ts < len(p1_stages):
                        p1_stages[ts]()
                for s in p1_stages[16:]:
                    s()
                projm["v"] = vm_last
        if DEBUG_TAPS and sg == 0:
            A.op("gp", lambda e: e.dma_start(out=DBG["q"], in_=Qsg), sem="sdt",
                 n=16, waits=[projm["q"]])
            A.op("gp", lambda e: e.dma_start(out=DBG["k"], in_=Ksg), sem="sdt",
                 n=16, waits=[projm["k"]])
            A.op("gp", lambda e: e.dma_start(out=DBG["va"], in_=Vaug), sem="sdt",
                 n=16, waits=[projm["v"]])
            mbst["dbgqkv"] = M("sdt")
        slot_last[f"wqkv{(3 * sg + 2) % 2}"] = projm["v"]


        for h in range(4):
            gh = sg * 4 + h
            oc = h // 2
            r0 = (h % 2) * 64
            par = h
            bm31_m = mbst[f"bm31m_{h}"]
            for qh in range(2):
                pv_last = None
                pend = []

                def emit_pv(c, eb, em, h=h):
                    nonlocal pv_last
                    for qb in range(2):
                        w = [em]
                        if c == 0:
                            w += [projm["v"]] + bank_waits(5 + qb)
                        is_l = (c == 15)
                        A.op("pe", lambda e, c=c, qb=qb, h=h, eb=eb,
                             is_l=is_l: e.matmul(
                            pAll[0:65, (5 + qb) * 512:(5 + qb) * 512 + 512],
                            Vaug[:, c, h * 65: h * 65 + 65],
                            eb[:, qb * 512:(qb + 1) * 512],
                            start=(c == 0), stop=is_l),
                            sem="sp" if qb == 1 else None,
                            waits=w)
                    pv_m = M("sp")
                    slot_last[f"E{c % 2}"] = pv_m
                    if c == 15:
                        pv_last = pv_m

                def make_norm_stages(pv_last, gh=gh, r0=r0, qh=qh):
                    st = {}

                    def stage_a():
                        A.op("dve", lambda e: e.reciprocal(
                            scrA, pAll[64:65, 5 * 512: 5 * 512 + 512]),
                            sem="sv", waits=[pv_last, mbst.get("rrow_guard")])
                        A.op("dve", lambda e: e.tensor_copy(
                            rrow_bf[0:1, 0:512], scrA), sem="sv")
                        A.op("dve", lambda e: e.reciprocal(
                            scrA, pAll[64:65, 6 * 512: 6 * 512 + 512]), sem="sv")
                        A.op("dve", lambda e: e.tensor_copy(
                            rrow_bf[0:1, 512:1024], scrA), sem="sv")
                        st["rm"] = M("sv")

                    def stage_b():
                        A.op("pe", lambda e: e.matmul(
                            bank(0), onesrz, rrow_bf[:, 0:512],
                            start=True, stop=True),
                             sem="sp", waits=[st["rm"]] + bank_waits(0))
                        rbm = M("sp")
                        A.op("act", lambda e: e.activation(
                            rb_sb[:, 0:512], bank(0), AF.Copy),
                            sem="sa", waits=[rbm])
                        bank_last[0] = M("sa")
                        st["rb0"] = bank_last[0]

                    def stage_c():
                        A.op("pe", lambda e: e.matmul(
                            bank(0), onesrz, rrow_bf[:, 512:1024],
                            start=True, stop=True),
                             sem="sp", waits=[st["rm"]] + bank_waits(0))
                        rbm = M("sp")
                        mbst["rrow_guard"] = rbm
                        A.op("act", lambda e: e.activation(
                            rb_sb[:, 512:1024], bank(0), AF.Copy),
                            sem="sa", waits=[rbm])
                        bank_last[0] = M("sa")
                        st["rb1"] = bank_last[0]

                    def stage_d():
                        fc = gh // 2
                        A.op("dve", lambda e, fc=fc, r0=r0, qh=qh: e.tensor_mul(
                            abuf[r0:r0 + 64, fc, qh * 1024: qh * 1024 + 1024],
                            pAll[0:64, 5 * 512: 5 * 512 + 1024],
                            rb_sb[0:64, 0:1024]), sem="sv",
                            waits=[st["rb0"], st["rb1"], pv_last])
                        ev = M("sv")
                        bank_last[5] = ev
                        bank_last[6] = ev
                        mbst["aev"] = ev

                    return [stage_a, stage_b, stage_c, stage_d]

                for c in range(16):
                    sb = 1 + (c % 2) * 2
                    eb = Ebuf[c % 2]
                    pn = mbst.get("pending_norm")
                    if pn:
                        pn.pop(0)()
                        if not pn:
                            mbst.pop("pending_norm")
                            aev = mbst["aev"]
                    if len(pend) >= 2:
                        if pend[0][0] == 0:
                            pn = mbst.get("pending_norm")
                            while pn:
                                pn.pop(0)()
                            if "pending_norm" in mbst:
                                mbst.pop("pending_norm")
                                aev = mbst["aev"]
                        emit_pv(*pend.pop(0))
                    blocks = {0: [], 1: []}
                    for j in (0, 1):
                        qbl = c + j - qh * 8
                        if 0 <= qbl < 8:
                            blocks[qbl // 4].append((j, qbl % 4))
                    for qb in range(2):
                        bia = blocks[qb]
                        kw = ([mbst[f"kpad{h % 2}"]]
                              if (qh == 0 and c == 0 and qb == 0) else [])
                        A.op("pe", lambda e, c=c, qb=qb, sb=sb, qh=qh, oc=oc,
                             hp=h % 2, stp=(len(bia) == 0): e.matmul(
                                 bank(sb + qb),
                                 Kpad[:, hp, c * 128:(c + 1) * 128],
                                 Qsg[:, oc,
                                     qh * 1024 + qb * 512:
                                     qh * 1024 + qb * 512 + 512],
                                 start=True, stop=stp),
                            sem="sp" if (len(bia) == 0 and qb == 1) else None,
                            waits=(bank_waits(sb + qb) + kw) if c < 2 else kw)
                        for bidx, (j, qp) in enumerate(bia):
                            lastb = (bidx == len(bia) - 1)
                            A.op("pe", lambda e, j=j, qp=qp, sb=sb, qb=qb,
                                 sg=sg, h=h, lastb=lastb: e.matmul(
                                     pAll[:, (sb + qb) * 512 + qp * 128:
                                          (sb + qb) * 512 + qp * 128 + 128],
                                     idb, toep[:, sg % 2, h,
                                               128 * j:128 * j + 128],
                                     start=False, stop=lastb),
                                sem="sp" if (lastb and qb == 1) else None,
                                waits=[mbst[f"toep_{sg}"]]
                                if (h == 0 and qh == 0 and c == 0) else [])
                    smm = M("sp")
                    split = max(0, min(1024, (c + 2 - qh * 8) * 128))
                    ew = [smm, bm31_m, slot_last.get(f"E{c % 2}")]
                    if split > 0:
                        A.op("act", lambda e, sb=sb, eb=eb, split=split,
                             par=par: e.activation(
                                 eb[:, 0:split],
                                 pAll[:, sb * 512: sb * 512 + split],
                                 AF.Exp, bias=bm[par][:, 0:1]), sem="sa",
                            waits=ew)
                    if split < 1024:
                        A.op("act", lambda e, sb=sb, eb=eb, split=split,
                             par=par: e.activation(
                                 eb[:, split:1024],
                                 pAll[:, sb * 512 + split: sb * 512 + 1024],
                                 AF.Exp, bias=bm31[par][:, 0:1]), sem="sa",
                            waits=ew if split == 0 else [])
                    em = M("sa")
                    if gh == sg * 4 + 3 and qh == 0 and c == 0:
                        mbst["exp_probe"] = em
                    if DEBUG_TAPS and gh == 0 and qh == 0 and c == 15:
                        A.op("gp", lambda e, eb=eb: e.dma_start(out=DBG["e0"], in_=eb),
                             sem="sdt", n=16, waits=[em])
                        slot_last[f"E{c % 2}"] = M("sdt")
                        mbst["dbge0"] = slot_last[f"E{c % 2}"]
                    exp_last[par] = em
                    bank_last[sb] = em
                    bank_last[sb + 1] = em
                    pend.append((c, eb, em))
                while pend:
                    emit_pv(*pend.pop(0))
                mbst["pending_norm"] = make_norm_stages(pv_last)
                mbst["last_pv"] = pv_last
            mbst[f"smm_par{h % 2}"] = smm
            if h < 2:
                emit_kpad(h + 2)

        mbst[f"toepu{sg % 2}"] = smm
        if sg < 6:
            emit_toep(sg + 2)
        slot_last["Vaug"] = aev

    if mbst.get("pending_norm") is not None:
        for f in mbst.pop("pending_norm"):
            f()
        aev = mbst["aev"]
    if DEBUG_TAPS:
        A.op("gp", lambda e: e.dma_start(out=DBG["ab"], in_=abuf), sem="sdt",
             n=16, waits=[aev])
        mbst["dbgab"] = M("sdt")
    a_done = aev
    for grp in ([0, 1, 2, 3, 4, 5, 6], [7, 8, 9, 10, 11, 12, 13], [14, 15]):
        for tb in range(NTB):
            mm_m = {}
            for fc in range(NC):
                buf = wqkv[fc % 2]
                osem = f"swq{fc % 2}"
                A.op("gp", lambda e, fc=fc, g0=grp[0], gl=len(grp), buf=buf:
                     e.dma_start(
                         out=buf[:, 0:gl * 128],
                         in_=dt_wo[fc][:, g0 * 128: g0 * 128 + gl * 128]),
                     sem=osem, n=16, waits=[slot_last.get(f"wqkv{fc % 2}")])
                wom = M(osem)
                for bi in range(len(grp)):
                    w = [wom]
                    if fc == 0:
                        w += bank_waits(bi) + [a_done]
                        if DEBUG_TAPS:
                            w += [tap1_done]
                    is_l = (bi == len(grp) - 1)
                    A.op("pe", lambda e, fc=fc, bi=bi, tb=tb, buf=buf: e.matmul(
                        bank(bi), buf[:, bi * 128: bi * 128 + 128],
                        abuf[:, fc, tb * 512:(tb + 1) * 512],
                        start=(fc == 0), stop=(fc == NC - 1)),
                        sem="sp" if is_l else None,
                        waits=w)
                    if is_l:
                        mk = M("sp")
                        slot_last[f"wqkv{fc % 2}"] = mk
                        if fc == NC - 1:
                            mm_all = M("sp") if False else mk
            for bi, ocg in enumerate(grp):
                A.op("dve", lambda e, bi=bi, ocg=ocg, tb=tb: e.tensor_add(
                    xbuf[:, ocg, tb * 512:(tb + 1) * 512], bank(bi),
                    xbuf[:, ocg, tb * 512:(tb + 1) * 512]), sem="sv",
                    waits=[mm_all])
                bank_last[bi] = M("sv")
                if tb == NTB - 1:
                    mbst[f"oev_{ocg}"] = bank_last[bi]
    stats_pass("ln3", (Ebuf[0], Ebuf[1]),
               src_ready={dc: mbst[f"oev_{dc}"] for dc in range(NC)})
    with nc.Block("b2") as blk:
        A.emit(blk)

    at_es.close()
    f2 = ExitStack()
    m_b2 = f2.enter_context(nc.sbuf_tensor("m_b2", [128, S], BF16)).ap()
    hbuf2 = f2.enter_context(nc.sbuf_tensor("hbuf2", [128, NHC * 512], BF16)).ap()
    wsl2 = [f2.enter_context(nc.sbuf_tensor(f"w2sl{i}", [128, 4096], BF16)).ap()
            for i in range(4)]
    xn2 = f2.enter_context(nc.sbuf_tensor("xn2", [128, 8192], BF16)).ap()
    ut2 = [f2.enter_context(nc.sbuf_tensor(f"u2t{i}", [128, 512], BF16)).ap()
           for i in range(2)]
    st2 = [f2.enter_context(nc.sbuf_tensor(f"st2{i}", [128, 512], F32)).ap()
           for i in range(6)]
    cs2 = f2.enter_context(nc.sbuf_tensor("cs2", [128, 512], BF16)).ap()
    hh2 = f2.enter_context(nc.sbuf_tensor("hh2", [128, 512], BF16)).ap()
    hl2 = f2.enter_context(nc.sbuf_tensor("hl2", [128, 512], BF16)).ap()
    xn32b = f2.enter_context(nc.sbuf_tensor("xn32b", [128, 512], F32)).ap()

    A.new_block()
    if DEBUG_TAPS:
        for dc in range(NC):
            A.op("gp", lambda e, dc=dc: e.dma_start(out=dt_tap2[dc],
                                                    in_=xbuf[:, dc, :]),
                 sem="sdt", n=16)
    for k in ("wsl0", "wsl1", "wsl2", "wsl3", "utmp0", "utmp1", "xn",
              "stage0", "stage1", "stage2", "stage3", "stage4", "stage5", "stage6", "stage7", "Vaug", "wqkv0", "wqkv1",
              "E0", "E1"):
        slot_last.pop(k, None)
    for buf_ in (cs2, hh2, hl2):
        A.op("dve", lambda e, buf_=buf_: e.memset(buf_, 0.0), sem="sv")
    bcast_mb("ln3", m_b2)
    ffn_pass("ln3", (m_b2, hbuf2, wsl2, xn2, ut2, st2, cs2, hh2, hl2, xn32b), final=True)
    A.op("sy", lambda e: e.nop(),
         waits=[(f"so{i}", A.cnt[f"so{i}"]) for i in range(8)])
    with nc.Block("b3") as blk:
        A.emit(blk)

    f2.close()
    return nc


_CACHED = {}


def _host_prep(inputs):
    def lowbyte(a):
        a = np.ascontiguousarray(a.astype(np.int32, copy=False))
        return a.view(np.uint8)[:, 0::4]

    wg8 = lowbyte(inputs["L0_ffn_wq_gate"])
    wu8 = lowbyte(inputs["L0_ffn_wq_up"])
    wd8 = lowbyte(inputs["L0_ffn_wq_down"])

    G = np.ascontiguousarray(wg8.T)
    wg = np.ascontiguousarray(
        G.reshape(16, 128, NPR, 256).transpose(2, 1, 0, 3))
    U = np.ascontiguousarray(wu8.T)
    wu = np.ascontiguousarray(
        U.reshape(16, 128, NPR, 256).transpose(2, 1, 0, 3))
    Dw = np.ascontiguousarray(wd8.T)
    wd = np.ascontiguousarray(
        Dw.reshape(NPR, 2, 128, D).transpose(0, 2, 1, 3))

    def attw(W):
        return np.ascontiguousarray(
            W.reshape(16, 128, 8, 256).transpose(2, 1, 0, 3)).astype(np.float32)

    shared = {
        "wg": wg, "wu": wu, "wd": wd,
        "wqa": attw(inputs["L0_attn_Wq"]),
        "wka": attw(inputs["L0_attn_Wk"]),
        "wva": attw(inputs["L0_attn_Wv"]),
        "woa": np.ascontiguousarray(
            inputs["L0_attn_Wo"].reshape(16, 128, D)).astype(np.float32),
        "relb": np.ascontiguousarray(inputs["L0_attn_rel_bias"]).astype(np.float32),
        "sgsd": np.ascontiguousarray(
            inputs["L0_ffn_s_gate"].reshape(NHC, 128).T).astype(np.float32),
        "susd": np.ascontiguousarray(
            inputs["L0_ffn_s_up"].reshape(NHC, 128).T).astype(np.float32),
        "sdsd": np.ascontiguousarray(
            inputs["L0_ffn_s_down"].reshape(NC, 128).T).astype(np.float32),
        "idbd": np.eye(128, dtype=np.float32).astype(BFD),
        "ones128d": np.ones((128, 1), dtype=BFD),
        "onesrowd": np.ones((1, 128), dtype=BFD),
        "neg128rd": np.full((1, 128), -128.0, dtype=BFD),
        "onehotd": _host_onehot_win(),
    }
    return shared


def kernel(**inputs):
    x = inputs["L0_attn_hidden_states"]
    B = x.shape[0]
    shared = _host_prep(inputs)
    if "nc" not in _CACHED:
        _CACHED["nc"] = build_program()
    nc = _CACHED["nc"]
    in_maps = []
    for b in range(B):
        m = dict(shared)
        m["x"] = np.ascontiguousarray(x[b].T.reshape(NC, 128, S)).astype(np.float32)
        in_maps.append(m)
    res = run_bass_kernel_spmd(nc, in_maps, core_ids=list(range(B)))
    out = np.empty((B, S, D), dtype=np.float32)
    for b in range(B):
        o = res.results[b]["out"]
        out[b] = o.transpose(2, 3, 0, 1).reshape(S, D)
    return out
